# version 02
# baseline (speedup 1.0000x reference)
import numpy as np
from contextlib import ExitStack
import concourse.bass as bass
import concourse.mybir as mybir
from concourse.bass_utils import run_bass_kernel_spmd

F32 = mybir.dt.float32
BF16 = mybir.dt.bfloat16
U32 = mybir.dt.uint32
AF = mybir.ActivationFunctionType
ALU = mybir.AluOpType
AX = mybir.AxisListType.X

NCORES = 8
TL = 2048
T = 16384
DM = 1024
DI = 2048
DTR = 64
NS = 16
L = 8192
DSL = 512
CAP = 2560
NGRP = CAP // 128
TRASH = T
LNEPS = 1e-5
RMSEPS = 1.1920929e-07

ENGS = ("T", "V", "S", "G", "Q")


class Prog:

    def __init__(self):
        self.steps = []
        self.keys = [(e, k) for e in ENGS for k in ("c", "d")]
        self.count = {k: 0 for k in self.keys}

    def add(self, eng, fn, deps=(), dma=False):
        key = (eng, "d" if dma else "c")
        inc = 16 if dma else 1
        deps = [d for d in deps if d is not None]
        if dma and self.count[key] > 0:
            deps.append((key, self.count[key]))
        self.count[key] += inc
        self.steps.append((eng, key, fn, tuple(deps), inc))
        return (key, self.count[key])

    def reset_gdma(self, deps=()):
        deps = [d for d in deps if d is not None]
        if self.count[("G", "d")] > 0:
            deps.append((("G", "d"), self.count[("G", "d")]))
        tok = self.add("G", lambda e: e.dma_reset(), deps=deps)
        self.count[("G", "d")] = 0
        return tok

    def build(self, nc, block):
        per_eng = {e: [] for e in ENGS}
        for eng, key, fn, deps, inc in self.steps:
            per_eng[eng].append((key, fn, deps, inc))
        sems = {k: nc.semaphore(f"sem_{k[0]}{k[1]}").__enter__() for k in self.keys}

        def make_section(e):
            def section(engine):
                waited = {k: 0 for k in self.keys}
                for key, fn, deps, inc in per_eng[e]:
                    for (dk, dv) in deps:
                        if dv > waited[dk]:
                            engine.wait_ge(sems[dk], dv)
                            waited[dk] = dv
                    ins = fn(engine)
                    ins.then_inc(sems[key], inc)
            return section

        for e, attr in (("T", "tensor"), ("V", "vector"), ("S", "scalar"),
                        ("G", "gpsimd"), ("Q", "sync")):
            getattr(block, attr)(make_section(e))


def ln_tile(p, xt, scratch, out_ap, deps, epsc, nf=DM, pr=128):
    sumt, sqt, ssqt, vart, meant, rstdt = scratch
    sumt = sumt[0:pr, :]; sqt = sqt[0:pr, 0:nf]; ssqt = ssqt[0:pr, :]
    vart = vart[0:pr, :]; meant = meant[0:pr, :]; rstdt = rstdt[0:pr, :]
    epsc_l = None
    d1 = p.add("V", lambda e: e.tensor_reduce(out=sumt, in_=xt, axis=AX, op=ALU.add), deps=deps)
    d2 = p.add("S", lambda e: e.activation(out=sqt, in_=xt, func=AF.Square), deps=deps)
    d3 = p.add("V", lambda e: e.tensor_reduce(out=ssqt, in_=sqt, axis=AX, op=ALU.add), deps=[d2])
    d4 = p.add("V", lambda e: e.tensor_scalar(out=meant, in0=sumt, scalar1=1.0 / nf, scalar2=None, op0=ALU.mult), deps=[d1])
    d5 = p.add("S", lambda e: e.activation(out=vart, in_=meant, func=AF.Square), deps=[d4])
    d6 = p.add("V", lambda e: e.scalar_tensor_tensor(out=vart, in0=ssqt, scalar=1.0 / nf, in1=vart, op0=ALU.mult, op1=ALU.subtract), deps=[d3, d5])
    d7 = p.add("S", lambda e: e.activation(out=rstdt, in_=vart, func=AF.Sqrt, bias=epsc[0:pr, :]), deps=[d6])
    d8 = p.add("V", lambda e: e.reciprocal(out=rstdt, in_=rstdt), deps=[d7])
    d9 = p.add("V", lambda e: e.tensor_scalar(out=out_ap, in0=xt, scalar1=meant, scalar2=rstdt, op0=ALU.subtract, op1=ALU.mult), deps=[d8] + list(deps))
    return d9


def const_cols(p, sb, values, name="cst"):
    t = sb(name, [128, len(values)], F32)
    deps = []
    aps = {}
    for i, v in enumerate(values):
        deps.append(p.add("V", (lambda i2, v2: lambda e: e.memset(t[:, i2:i2 + 1], v2))(i, v)))
        aps[v] = t[:, i:i + 1]
    return aps, deps[-1]


def ident_setup(p, nc, sb):
    iot = sb("iot_id", [128, 128], F32)
    identf = sb("identf", [128, 128], F32)
    ident = sb("ident", [128, 128], BF16)
    d_io = p.add("G", lambda e: e.iota(iot[:, :], pattern=[[1, 128]], base=0, channel_multiplier=-1, allow_small_or_imprecise_dtypes=True))
    d_idf = p.add("V", lambda e: e.tensor_scalar(out=identf[:, :], in0=iot[:, :], scalar1=0.0, scalar2=None, op0=ALU.is_equal), deps=[d_io])
    d_id = p.add("V", lambda e: e.tensor_copy(ident[:, :], identf[:, :]), deps=[d_idf])
    return ident, identf, d_id, d_idf


def build_L1():
    nc = bass.Bass("TRN2", target_bir_lowering=False)
    NT = TL + 128
    x_halo = nc.dram_tensor("x_halo", [NT, DM], F32, kind="ExternalInput").ap()
    w_inT = nc.dram_tensor("w_inT", [DM, 2 * DI], F32, kind="ExternalInput").ap()
    convwT = nc.dram_tensor("convwT", [4, DI], F32, kind="ExternalInput").ap()
    conv_b = nc.dram_tensor("conv_b", [1, DI], F32, kind="ExternalInput").ap()
    xprojT = nc.dram_tensor("xprojT", [DI, 96], F32, kind="ExternalInput").ap()
    dtwT = nc.dram_tensor("dtwT", [DTR, DI], F32, kind="ExternalInput").ap()
    dtb = nc.dram_tensor("dtb", [1, DI], F32, kind="ExternalInput").ap()

    u_out = nc.dram_tensor("u_out", [TL, DI], F32, kind="ExternalOutput").ap()
    z_buf = nc.dram_tensor("z_buf", [NT, DI], F32, kind="ExternalOutput").ap()
    delta_out = nc.dram_tensor("delta_out", [TL, DI], F32, kind="ExternalOutput").ap()
    bc_out = nc.dram_tensor("bc_out", [32, TL], F32, kind="ExternalOutput").ap()

    w_inT_bf = nc.dram_tensor("w_inT_bf", [DM, 2 * DI], BF16, kind="Internal").ap()
    xpre_d = nc.dram_tensor("xpre_d", [NT, DI], F32, kind="Internal").ap()

    p = Prog()
    with ExitStack() as stk:
        def sb(name, shape, dt):
            return stk.enter_context(nc.sbuf_tensor(name, shape, dt))

        def ps(name, shape, dt):
            return stk.enter_context(nc.psum_tensor(name, shape, dt))

        ones1 = sb("ones1", [1, 128], F32)
        wsb = sb("wsb", [128, 8 * 4096], BF16)
        cw = sb("cw", [128, 4 * DI], F32)
        cb = sb("cb", [128, DI], F32)
        db2 = sb("db2", [128, DI], F32)
        rowbuf = sb("rowbuf", [1, DI], F32)
        xprojsb = sb("xprojsb", [128, 16 * 96], BF16)
        dtwsb = sb("dtwsb", [DTR, DI], BF16)
        xt = sb("xt", [128, DM], F32)
        sumt = sb("sumt", [128, 1], F32)
        ssqt = sb("ssqt", [128, 1], F32)
        meant = sb("meant", [128, 1], F32)
        vart = sb("vart", [128, 1], F32)
        rstdt = sb("rstdt", [128, 1], F32)
        lnbf = sb("lnbf", [128, DM], BF16)
        lnT = sb("lnT", [128, DM], BF16)
        xz = sb("xz", [128, DI], F32)
        cx = [sb("cx0", [128, DI], F32)] * 2
        cacc = sb("cacc", [128, DI], F32)
        ut = sb("ut", [128, DI], F32)
        ubf = sb("ubf", [128, DI], BF16)
        uT = sb("uT", [128, DI], BF16)
        xdbl = sb("xdbl", [96, 128], F32)
        xdblbf = sb("xdblbf", [96, 128], BF16)
        vt = sb("vt", [128, DI], F32)
        s1 = sb("s1", [128, DI], F32)
        s2 = sb("s2", [128, DI], F32)
        sqt = s1
        ctmp = vt

        ptr = ps("ptr", [128, 128], BF16)
        pmm = ps("pmm", [128, 2048], F32)
        pxd = ps("pxd", [96, 128], F32)
        pcw = ps("pcw", [128, 512], F32)

        ident, identf, d_id, d_idf = None, None, None, None
        with nc.Block() as block:
            ident, identf, d_id, d_idf = ident_setup(p, nc, sb)
            d_ones = p.add("V", lambda e: e.memset(ones1[:], 1.0))
            csts, d_cst = const_cols(p, sb, [LNEPS, 1.0, -1.0, 2.0])

            d_wc = p.add("G", lambda e: e.dma_start(out=w_inT_bf, in_=w_inT), dma=True)
            d_dtw = p.add("G", lambda e: e.dma_start(out=dtwsb[:], in_=dtwT), dma=True)
            d_xp = [p.add("G", (lambda c: lambda e: e.dma_start(
                out=xprojsb[:, c * 96:(c + 1) * 96], in_=xprojT[c * 128:(c + 1) * 128, :]))(ch), dma=True) for ch in range(16) for c in [ch]]
            d_w = [p.add("G", (lambda c: lambda e: e.dma_start(
                out=wsb[:, c * 4096:(c + 1) * 4096], in_=w_inT_bf[c * 128:(c + 1) * 128, :]))(ch), deps=[d_wc], dma=True) for ch in range(8) for c in [ch]]

            last_evac = None
            d_row = None
            jobs = [(convwT[j:j + 1, :], cw, j * DI, 1.0) for j in range(4)]
            jobs.append((conv_b, cb, 0, 1.0))
            jobs.append((dtb, db2, 0, 2.0))
            for src_ap, dst, off, scale in jobs:
                d_row = p.add("Q", (lambda sa: lambda e: e.dma_start(out=rowbuf[:], in_=sa))(src_ap),
                              deps=[last_evac, d_cst], dma=True)
                for f4 in range(4):
                    d_mm = p.add("T", (lambda ff: lambda e: e.matmul(
                        pcw[:, :], ones1[:, :], rowbuf[0:1, ff * 512:(ff + 1) * 512],
                        start=True, stop=True))(f4), deps=[d_row, d_ones, last_evac])
                    last_evac = p.add("S", (lambda dd, oo, ff, ss: lambda e: e.activation(
                        out=dd[:, oo + ff * 512: oo + (ff + 1) * 512], in_=pcw[:, :], func=AF.Copy,
                        scale=csts[ss]))(dst, off, f4, scale), deps=[d_mm])
            d_setup = last_evac

            lnscr = (sumt, sqt, ssqt, vart, meant, rstdt)
            d_lastz = None
            d_lastx = None
            d_prev_dt = None
            for i in range(17):
                r0 = i * 128
                r1 = r0 + 128
                pr = 128
                d_x = p.add("Q", (lambda a, b: lambda e: e.dma_start(out=xt[0:b - a, :], in_=x_halo[a:b, :]))(r0, r1),
                            deps=[d_setup, d_lastz, d_lastx], dma=True)
                d_ln = ln_tile(p, xt[0:pr, :], lnscr, lnbf[0:pr, :], [d_x, d_cst], csts[LNEPS], pr=pr)
                d_tr = []
                lastev = None
                for c in range(8):
                    d_t = p.add("T", (lambda cc, pp: lambda e: e.transpose(
                        ptr[:, 0:pp], lnbf[0:pp, cc * 128:(cc + 1) * 128], ident[0:pp, 0:pp]))(c, pr), deps=[d_ln, d_id, lastev])
                    lastev = p.add("S", (lambda cc, pp: lambda e: e.copy(
                        out=lnT[:, cc * 128:cc * 128 + pp], in_=ptr[:, 0:pp]))(c, pr), deps=[d_t])
                    d_tr.append(lastev)
                d_ev = None
                for grp in range(2):
                    d_mm = None
                    for c in range(8):
                        for f in range(4):
                            d_mm = p.add("T", (lambda cc, ff, gg, pp: lambda e: e.matmul(
                                pmm[0:pp, ff * 512:(ff + 1) * 512],
                                lnT[:, cc * 128:cc * 128 + pp],
                                wsb[:, cc * 4096 + gg * 2048 + ff * 512: cc * 4096 + gg * 2048 + (ff + 1) * 512],
                                start=(cc == 0), stop=(cc == 7)))(c, f, grp, pr),
                                deps=[d_tr[c], d_ev, d_prev_dt] + (d_w if i == 0 and c == 0 else []))
                    d_ev = p.add("S", (lambda gg, pp: lambda e: e.copy(
                        out=xz[0:pp, :], in_=pmm[0:pp, :]))(grp, pr),
                        deps=[d_mm] + ([d_lastx] if grp == 1 else [d_lastz]))
                    if grp == 0:
                        d_lastx = p.add("Q", (lambda a, b: lambda e: e.dma_start(
                            out=xpre_d[a:b, :], in_=xz[0:b - a, :]))(r0, r1), deps=[d_ev], dma=True)
                    else:
                        d_lastz = p.add("Q", (lambda a, b: lambda e: e.dma_start(
                            out=z_buf[a:b, :], in_=xz[0:b - a, :]))(r0, r1), deps=[d_ev], dma=True)
                d_prev_dt = d_ev
            d_phaseA = [d_lastz, d_lastx]

            d_prev_out = None
            d_uo = None
            d_bco = None
            for i in range(16):
                deps0 = d_phaseA + [d_prev_out]
                d_cx = {}
                d_conv = None
                for j in range(4):
                    buf = cx[j % 2]
                    war = [d_conv] if j >= 1 else []
                    d_cx[j] = p.add("Q" if j % 2 == 0 else "G",
                                    (lambda ii, jj, bb: lambda e: e.dma_start(
                                        out=bb[:, :], in_=xpre_d[ii * 128 + jj: ii * 128 + jj + 128, :]))(i, j, buf),
                                    deps=deps0 + war, dma=True)
                    if j == 0:
                        d_conv = p.add("V", (lambda bb: lambda e: e.tensor_tensor(
                            out=cacc[:, :], in0=bb[:, :], in1=cw[:, 0:DI], op=ALU.mult))(buf), deps=[d_cx[0], d_setup])
                    else:
                        d_m = p.add("V", (lambda bb, jj: lambda e: e.tensor_tensor(
                            out=ctmp[:, :], in0=bb[:, :], in1=cw[:, jj * DI:(jj + 1) * DI], op=ALU.mult))(buf, j), deps=[d_cx[j], d_conv])
                        d_conv = p.add("V", lambda e: e.tensor_tensor(out=cacc[:, :], in0=cacc[:, :], in1=ctmp[:, :], op=ALU.add), deps=[d_m])
                d_conv = p.add("V", lambda e: e.tensor_tensor(out=cacc[:, :], in0=cacc[:, :], in1=cb[:, :], op=ALU.add), deps=[d_conv])
                d_u = p.add("S", lambda e: e.activation(out=ut[:, :], in_=cacc[:, :], func=AF.Silu), deps=[d_conv])
                d_uo = p.add("Q", (lambda ii: lambda e: e.dma_start(
                    out=u_out[ii * 128:(ii + 1) * 128, :], in_=ut[:, :]))(i), deps=[d_u], dma=True)
                d_ub = p.add("V", lambda e: e.tensor_copy(ubf[:, :], ut[:, :]), deps=[d_u])
                lastev = None
                d_trs = []
                for c in range(16):
                    d_t = p.add("T", (lambda cc: lambda e: e.transpose(
                        ptr[:, :], ubf[:, cc * 128:(cc + 1) * 128], ident[:, :]))(c), deps=[d_ub, d_id, lastev])
                    lastev = p.add("S", (lambda cc: lambda e: e.copy(
                        out=uT[:, cc * 128:(cc + 1) * 128], in_=ptr[:, :]))(c), deps=[d_t])
                    d_trs.append(lastev)
                d_xd = None
                for c in range(16):
                    d_xd = p.add("T", (lambda cc: lambda e: e.matmul(
                        pxd[:, :], xprojsb[:, cc * 96:(cc + 1) * 96], uT[:, cc * 128:(cc + 1) * 128],
                        start=(cc == 0), stop=(cc == 15)))(c), deps=[d_trs[c]] + (d_xp if i == 0 and c == 0 else []))
                d_xe = p.add("S", lambda e: e.copy(out=xdbl[:, :], in_=pxd[:, :]), deps=[d_xd])
                d_bco = p.add("Q", (lambda ii: lambda e: e.dma_start(
                    out=bc_out[:, ii * 128:(ii + 1) * 128], in_=xdbl[64:96, :]))(i), deps=[d_xe], dma=True)
                d_xb = p.add("V", lambda e: e.tensor_copy(xdblbf[0:64, :], xdbl[0:64, :]), deps=[d_xe])
                d_dt = None
                for f in range(4):
                    d_dt = p.add("T", (lambda ff: lambda e: e.matmul(
                        pmm[:, ff * 512:(ff + 1) * 512], xdblbf[0:64, :], dtwsb[:, ff * 512:(ff + 1) * 512],
                        start=True, stop=True))(f), deps=[d_xb, d_dtw, d_prev_out])
                d_v = p.add("V", lambda e: e.tensor_tensor(out=vt[:, :], in0=pmm[:, :], in1=db2[:, :], op=ALU.add), deps=[d_dt, d_setup])
                d_a = p.add("S", lambda e: e.activation(out=s1[:, :], in_=vt[:, :], func=AF.Abs), deps=[d_v])
                d_e = p.add("S", lambda e: e.activation(out=s2[:, :], in_=s1[:, :], func=AF.Exp, scale=csts[-1.0]), deps=[d_a])
                d_l = p.add("S", lambda e: e.activation(out=s1[:, :], in_=s2[:, :], func=AF.Ln, bias=csts[1.0]), deps=[d_e])
                d_r = p.add("V", lambda e: e.tensor_scalar(out=s2[:, :], in0=vt[:, :], scalar1=0.0, scalar2=None, op0=ALU.max), deps=[d_l, d_v])
                d_d = p.add("V", lambda e: e.tensor_tensor(out=cacc[:, :], in0=s2[:, :], in1=s1[:, :], op=ALU.add), deps=[d_r, d_l])
                d_prev_out = p.add("Q", (lambda ii: lambda e: e.dma_start(
                    out=delta_out[ii * 128:(ii + 1) * 128, :], in_=cacc[:, :]))(i), deps=[d_d], dma=True)

            p.add("G", lambda e: e.engine_nop(), deps=[d_prev_out, d_uo, d_bco])
            p.build(nc, block)
    return nc


def build_L2():
    nc = bass.Bass("TRN2", target_bir_lowering=False)
    dfm = nc.dram_tensor("dfm", [DSL, L], F32, kind="ExternalInput").ap()
    ufm = nc.dram_tensor("ufm", [DSL, L], F32, kind="ExternalInput").ap()
    zfm = nc.dram_tensor("zfm", [DSL, L], F32, kind="ExternalInput").ap()
    bct = nc.dram_tensor("bct", [32, L], F32, kind="ExternalInput").ap()
    alog = nc.dram_tensor("alog", [DSL, NS], F32, kind="ExternalInput").ap()
    dsl = nc.dram_tensor("dsl", [DSL, 1], F32, kind="ExternalInput").ap()
    woutT = nc.dram_tensor("woutT", [DSL, DM], F32, kind="ExternalInput").ap()

    attn_part = nc.dram_tensor("attn_part", [L, DM], F32, kind="ExternalOutput").ap()

    Q = 1024
    NQ = L // Q

    p = Prog()
    with ExitStack() as stk:
        def sb(name, shape, dt):
            return stk.enter_context(nc.sbuf_tensor(name, shape, dt))

        def ps(name, shape, dt):
            return stk.enter_context(nc.psum_tensor(name, shape, dt))

        ones1 = sb("ones1", [1, 128], F32)
        rows = sb("rows", [1, 2 * L], F32)
        A_sb = sb("A_sb", [128, 4 * NS], F32)
        D_sb = sb("D_sb", [128, 4], F32)
        wo_sb = sb("wo_sb", [128, 4 * DM], BF16)
        carry = sb("carry", [128, 4 * NS], F32)
        dq = sb("dq", [128, Q], F32)
        uq = sb("uq", [128, Q], F32)
        zq = sb("zq", [128, Q], F32)
        wq = sb("wq", [128, Q], F32)
        yq = sb("yq", [128, Q], F32)
        dAq = sb("dAq", [128, Q], F32)
        dbu = sb("dbu", [128, Q], F32)
        stq = sb("stq", [128, Q], F32)
        tmpq = sb("tmpq", [128, Q], F32)
        sgq = sb("sgq", [128, Q], F32)
        ybf = sb("ybf", [128, 4 * L], BF16)
        oev = sb("oev", [128, DM], F32)

        pB = ps("pB", [128, Q], F32)
        pC = ps("pC", [128, Q], F32)
        pO = ps("pO", [128, DM], F32)

        with nc.Block() as block:
            d_ones = p.add("V", lambda e: e.memset(ones1[:], 1.0))
            d_wo = p.add("G", lambda e: e.dma_start(out=wo_sb[:, :].rearrange("p (g m) -> p g m", g=4),
                                                    in_=woutT.rearrange("(g p) m -> p g m", p=128)), dma=True)
            d_al = p.add("Q", lambda e: e.dma_start(out=A_sb[:, :].rearrange("p (g n) -> p g n", g=4),
                                                    in_=alog.rearrange("(g p) n -> p g n", p=128)), dma=True)
            d_dl = p.add("Q", lambda e: e.dma_start(out=D_sb[:, :].rearrange("p (g o) -> p g o", o=1),
                                                    in_=dsl.rearrange("(g p) o -> p g o", p=128),
                                                    allow_slow_non_contiguous=True), dma=True)
            d_A = p.add("S", lambda e: e.activation(out=A_sb[:, :], in_=A_sb[:, :], func=AF.Exp), deps=[d_al])
            d_A = p.add("V", lambda e: e.tensor_scalar(out=A_sb[:, :], in0=A_sb[:, :], scalar1=-1.0, scalar2=None, op0=ALU.mult), deps=[d_A])
            d_cz = p.add("V", lambda e: e.memset(carry[:], 0.0))

            d_prev = {}
            d_rows = None
            d_db_prev = None
            d_tm_prev = None
            for g in range(4):
                for q in range(NQ):
                    deps0 = [d_A, d_cz, d_prev.get((g, q - 1)), d_prev.get((g - 1, NQ - 1))]
                    d_ld = p.add("Q", (lambda gg, qq: lambda e: e.dma_start(
                        out=dq[:, :], in_=dfm[gg * 128:(gg + 1) * 128, qq * Q:(qq + 1) * Q]))(g, q), deps=deps0, dma=True)
                    d_lu = p.add("Q", (lambda gg, qq: lambda e: e.dma_start(
                        out=uq[:, :], in_=ufm[gg * 128:(gg + 1) * 128, qq * Q:(qq + 1) * Q]))(g, q), deps=deps0, dma=True)
                    d_lz = p.add("G", (lambda gg, qq: lambda e: e.dma_start(
                        out=zq[:, :], in_=zfm[gg * 128:(gg + 1) * 128, qq * Q:(qq + 1) * Q]))(g, q), deps=deps0, dma=True)
                    d_w = p.add("V", lambda e: e.tensor_tensor(out=wq[:, :], in0=dq[:, :], in1=uq[:, :], op=ALU.mult), deps=[d_ld, d_lu])
                    d_y = p.add("V", (lambda gg: lambda e: e.tensor_scalar(
                        out=yq[:, :], in0=uq[:, :], scalar1=D_sb[:, gg:gg + 1], scalar2=None, op0=ALU.mult))(g), deps=[d_lu, d_dl])
                    for n in range(NS):
                        if q == 0:
                            d_rows = p.add("G", (lambda nn: lambda e: e.dma_start(
                                out=rows[:, :].rearrange("p (o l) -> p o l", o=2),
                                in_=bct.rearrange("(h n) l -> n h l", n=16)[nn:nn + 1, :, :]))(n),
                                deps=[d_db_prev, d_tm_prev], dma=True)
                            d_prev[("rows", g, n)] = d_rows
                        d_rows_n = d_prev[("rows", g, n)]
                        d_dA = p.add("S", (lambda gg, nn: lambda e: e.activation(
                            out=dAq[:, :], in_=dq[:, :], func=AF.Exp,
                            scale=A_sb[:, gg * NS + nn:gg * NS + nn + 1]))(g, n),
                            deps=[d_ld, d_A, d_prev.get(("sc", g, q, n - 1))])
                        d_mB = p.add("T", (lambda qq: lambda e: e.matmul(
                            pB[:, 0:512], ones1[:, :], rows[0:1, qq * Q: qq * Q + 512],
                            start=True, stop=True))(q), deps=[d_rows_n, d_ones, d_db_prev])
                        d_mB = p.add("T", (lambda qq: lambda e: e.matmul(
                            pB[:, 512:1024], ones1[:, :], rows[0:1, qq * Q + 512: qq * Q + 1024],
                            start=True, stop=True))(q), deps=[d_mB])
                        d_db = p.add("V", lambda e: e.tensor_tensor(out=dbu[:, :], in0=wq[:, :], in1=pB[:, :], op=ALU.mult), deps=[d_w, d_mB])
                        d_db_prev = d_db
                        d_sc = p.add("V", (lambda gg, nn: lambda e: e.tensor_tensor_scan(
                            out=stq[:, :], data0=dAq[:, :], data1=dbu[:, :],
                            initial=carry[:, gg * NS + nn:gg * NS + nn + 1],
                            op0=ALU.mult, op1=ALU.add))(g, n), deps=[d_dA, d_db])
                        d_prev[("sc", g, q, n)] = d_sc
                        d_cu = p.add("V", (lambda gg, nn: lambda e: e.tensor_copy(
                            carry[:, gg * NS + nn:gg * NS + nn + 1], stq[:, Q - 1:Q]))(g, n), deps=[d_sc])
                        d_mC = p.add("T", (lambda qq: lambda e: e.matmul(
                            pC[:, 0:512], ones1[:, :], rows[0:1, L + qq * Q: L + qq * Q + 512],
                            start=True, stop=True))(q), deps=[d_rows_n, d_ones, d_tm_prev])
                        d_mC = p.add("T", (lambda qq: lambda e: e.matmul(
                            pC[:, 512:1024], ones1[:, :], rows[0:1, L + qq * Q + 512: L + qq * Q + 1024],
                            start=True, stop=True))(q), deps=[d_mC])
                        d_tm = p.add("V", lambda e: e.tensor_tensor(out=tmpq[:, :], in0=stq[:, :], in1=pC[:, :], op=ALU.mult), deps=[d_sc, d_mC])
                        d_tm_prev = d_tm
                        d_y = p.add("V", lambda e: e.tensor_tensor(out=yq[:, :], in0=yq[:, :], in1=tmpq[:, :], op=ALU.add), deps=[d_tm, d_y])
                    d_sg = p.add("S", lambda e: e.activation(out=sgq[:, :], in_=zq[:, :], func=AF.Sigmoid), deps=[d_lz])
                    d_sz = p.add("V", lambda e: e.tensor_tensor(out=sgq[:, :], in0=zq[:, :], in1=sgq[:, :], op=ALU.mult), deps=[d_sg])
                    d_yg = p.add("V", lambda e: e.tensor_tensor(out=tmpq[:, :], in0=yq[:, :], in1=sgq[:, :], op=ALU.mult), deps=[d_y, d_sz])
                    d_prev[(g, q)] = p.add("V", (lambda gg, qq: lambda e: e.tensor_copy(
                        ybf[:, gg * L + qq * Q: gg * L + (qq + 1) * Q], tmpq[:, :]))(g, q), deps=[d_yg])

            d_done = [d_prev[(g, NQ - 1)] for g in range(4)]
            d_oprev = None
            for tt in range(64):
                d_mm = None
                for g in range(4):
                    for h in range(2):
                        d_mm = p.add("T", (lambda gg, hh, t2: lambda e: e.matmul(
                            pO[:, hh * 512:(hh + 1) * 512],
                            ybf[:, gg * L + t2 * 128: gg * L + t2 * 128 + 128],
                            wo_sb[:, gg * DM + hh * 512: gg * DM + (hh + 1) * 512],
                            start=(gg == 0), stop=(gg == 3)))(g, h, tt),
                            deps=d_done + [d_wo, d_oprev])
                d_ev = p.add("S", lambda e: e.copy(out=oev[:, :], in_=pO[:, :]), deps=[d_mm])
                d_oprev = p.add("Q", (lambda t2: lambda e: e.dma_start(
                    out=attn_part[t2 * 128:(t2 + 1) * 128, :], in_=oev[:, :]))(tt), deps=[d_ev], dma=True)

            p.add("G", lambda e: e.engine_nop(), deps=[d_oprev])
            p.build(nc, block)
    return nc


def build_L3():
    nc = bass.Bass("TRN2", target_bir_lowering=False)
    attn4 = nc.dram_tensor("attn4", [4, TL, DM], F32, kind="ExternalInput").ap()
    x_loc = nc.dram_tensor("x_loc", [TL, DM], F32, kind="ExternalInput").ap()
    rwT = nc.dram_tensor("rwT", [DM, 8], F32, kind="ExternalInput").ap()

    xmid_out = nc.dram_tensor("xmid_out", [TL, DM], F32, kind="ExternalOutput").ap()
    xf_out = nc.dram_tensor("xf_out", [TL, DM], BF16, kind="ExternalOutput").ap()
    logitsT_out = nc.dram_tensor("logitsT_out", [8, TL], F32, kind="ExternalOutput").ap()

    p = Prog()
    with ExitStack() as stk:
        def sb(name, shape, dt):
            return stk.enter_context(nc.sbuf_tensor(name, shape, dt))

        def ps(name, shape, dt):
            return stk.enter_context(nc.psum_tensor(name, shape, dt))

        rw_sb = sb("rw_sb", [128, 8 * 8], BF16)
        a0 = sb("a0", [128, DM], F32)
        a1 = sb("a1", [128, DM], F32)
        xt = sb("xt", [128, DM], F32)
        lnt = sb("lnt", [128, DM], F32)
        xft = sb("xft", [128, DM], F32)
        xfb = sb("xfb", [128, DM], BF16)
        xnb = sb("xnb", [128, DM], BF16)
        xnT = sb("xnT", [128, DM], BF16)
        sumt = sb("sumt", [128, 1], F32)
        sqt = sb("sqt", [128, DM], F32)
        ssqt = sb("ssqt", [128, 1], F32)
        meant = sb("meant", [128, 1], F32)
        vart = sb("vart", [128, 1], F32)
        rstdt = sb("rstdt", [128, 1], F32)
        st_ = sb("st_", [128, 1], F32)
        lgev = sb("lgev", [8, 128], F32)

        ptr = ps("ptr", [128, 128], BF16)
        plg = ps("plg", [8, 128], F32)

        with nc.Block() as block:
            ident, identf, d_id, d_idf = ident_setup(p, nc, sb)
            csts, d_cst = const_cols(p, sb, [LNEPS])
            d_rw = p.add("G", lambda e: e.dma_start(out=rw_sb[:, :].rearrange("p (c o) -> p c o", c=8),
                                                    in_=rwT.rearrange("(c p) o -> p c o", p=128)), dma=True)

            lnscr = (sumt, sqt, ssqt, vart, meant, rstdt)
            d_prev = None
            for i in range(16):
                deps0 = [d_prev]
                d_l0 = p.add("Q", (lambda ii: lambda e: e.dma_start(out=a0[:, :], in_=attn4[0, ii * 128:(ii + 1) * 128, :]))(i), deps=deps0, dma=True)
                d_l1 = p.add("G", (lambda ii: lambda e: e.dma_start(out=a1[:, :], in_=attn4[1, ii * 128:(ii + 1) * 128, :]))(i), deps=deps0, dma=True)
                d_s = p.add("V", lambda e: e.tensor_tensor(out=a0[:, :], in0=a0[:, :], in1=a1[:, :], op=ALU.add), deps=[d_l0, d_l1])
                d_l2 = p.add("G", (lambda ii: lambda e: e.dma_start(out=a1[:, :], in_=attn4[2, ii * 128:(ii + 1) * 128, :]))(i), deps=[d_s], dma=True)
                d_s = p.add("V", lambda e: e.tensor_tensor(out=a0[:, :], in0=a0[:, :], in1=a1[:, :], op=ALU.add), deps=[d_l2])
                d_l3 = p.add("G", (lambda ii: lambda e: e.dma_start(out=a1[:, :], in_=attn4[3, ii * 128:(ii + 1) * 128, :]))(i), deps=[d_s], dma=True)
                d_s = p.add("V", lambda e: e.tensor_tensor(out=a0[:, :], in0=a0[:, :], in1=a1[:, :], op=ALU.add), deps=[d_l3])
                d_lx = p.add("Q", (lambda ii: lambda e: e.dma_start(out=xt[:, :], in_=x_loc[ii * 128:(ii + 1) * 128, :]))(i), deps=deps0, dma=True)
                d_ln = ln_tile(p, a0[:, :], lnscr, lnt[:, :], [d_s, d_cst], csts[LNEPS])
                d_xm = p.add("V", lambda e: e.tensor_tensor(out=xt[:, :], in0=xt[:, :], in1=lnt[:, :], op=ALU.add), deps=[d_ln, d_lx])
                d_xo = p.add("Q", (lambda ii: lambda e: e.dma_start(out=xmid_out[ii * 128:(ii + 1) * 128, :], in_=xt[:, :]))(i), deps=[d_xm], dma=True)
                d_l3n = ln_tile(p, xt[:, :], lnscr, xft[:, :], [d_xm], csts[LNEPS])
                d_fb = p.add("V", lambda e: e.tensor_copy(xfb[:, :], xft[:, :]), deps=[d_l3n])
                d_fo = p.add("Q", (lambda ii: lambda e: e.dma_start(out=xf_out[ii * 128:(ii + 1) * 128, :], in_=xfb[:, :]))(i), deps=[d_fb], dma=True)
                d_q = p.add("S", lambda e: e.activation(out=sqt[:, :], in_=xft[:, :], func=AF.Square), deps=[d_l3n])
                d_qs = p.add("V", lambda e: e.tensor_reduce(out=ssqt[:, :], in_=sqt[:, :], axis=AX, op=ALU.add), deps=[d_q])
                d_m = p.add("V", lambda e: e.tensor_scalar(out=st_[:, :], in0=ssqt[:, :], scalar1=1.0 / DM, scalar2=RMSEPS, op0=ALU.mult, op1=ALU.add), deps=[d_qs])
                d_sq = p.add("S", lambda e: e.activation(out=st_[:, :], in_=st_[:, :], func=AF.Sqrt), deps=[d_m])
                d_rc = p.add("V", lambda e: e.reciprocal(out=st_[:, :], in_=st_[:, :]), deps=[d_sq])
                d_xn = p.add("V", lambda e: e.tensor_scalar(out=xnb[:, :], in0=xft[:, :], scalar1=st_[:, :], scalar2=None, op0=ALU.mult), deps=[d_rc, d_l3n])
                lastev = None
                d_trs = []
                for c in range(8):
                    d_t = p.add("T", (lambda cc: lambda e: e.transpose(
                        ptr[:, :], xnb[:, cc * 128:(cc + 1) * 128], ident[:, :]))(c), deps=[d_xn, d_id, lastev])
                    lastev = p.add("S", (lambda cc: lambda e: e.copy(
                        out=xnT[:, cc * 128:(cc + 1) * 128], in_=ptr[:, 0:128]))(c), deps=[d_t])
                    d_trs.append(lastev)
                d_mm = None
                for c in range(8):
                    d_mm = p.add("T", (lambda cc: lambda e: e.matmul(
                        plg[:, :], rw_sb[:, cc * 8:(cc + 1) * 8], xnT[:, cc * 128:(cc + 1) * 128],
                        start=(cc == 0), stop=(cc == 7)))(c), deps=[d_trs[c], d_rw])
                d_le = p.add("S", lambda e: e.copy(out=lgev[:, :], in_=plg[:, :]), deps=[d_mm])
                d_prev = p.add("Q", (lambda ii: lambda e: e.dma_start(
                    out=logitsT_out[:, ii * 128:(ii + 1) * 128], in_=lgev[:, :]))(i), deps=[d_le, d_xo, d_fo], dma=True)

            p.add("G", lambda e: e.engine_nop(), deps=[d_prev])
            p.build(nc, block)
    return nc


def build_L4(debug=False):
    nc = bass.Bass("TRN2", target_bir_lowering=False)
    xf_full = nc.dram_tensor("xf_full", [T, DM], BF16, kind="ExternalInput").ap()
    logits_full = nc.dram_tensor("logits_full", [T, 8], F32, kind="ExternalInput").ap()
    e_col = nc.dram_tensor("e_col", [128, 1], F32, kind="ExternalInput").ap()
    rg_e = nc.dram_tensor("rg_e", [DM, 4096], F32, kind="ExternalInput").ap()
    ru_e = nc.dram_tensor("ru_e", [DM, 4096], F32, kind="ExternalInput").ap()
    rd_e = nc.dram_tensor("rd_e", [4096, DM], F32, kind="ExternalInput").ap()

    routed = nc.dram_tensor("routed", [T + 128, DM], F32, kind="ExternalOutput").ap()
    if debug:
        dbg_rank = nc.dram_tensor("dbg_rank", [128, 128], F32, kind="ExternalOutput").ap()
        dbg_we = nc.dram_tensor("dbg_we", [128, 128], F32, kind="ExternalOutput").ap()
        dbg_srk = nc.dram_tensor("dbg_srk", [128, 128], F32, kind="ExternalOutput").ap()
        dbg_tok = nc.dram_tensor("dbg_tok", [128, NGRP], F32, kind="ExternalOutput").ap()
        dbg_wsl = nc.dram_tensor("dbg_wsl", [128, NGRP], F32, kind="ExternalOutput").ap()
        dbg_am1 = nc.dram_tensor("dbg_am1", [128, 128], F32, kind="ExternalOutput").ap()
        dbg_am2 = nc.dram_tensor("dbg_am2", [128, 128], F32, kind="ExternalOutput").ap()
        dbg_xgT = nc.dram_tensor("dbg_xgT", [128, 8 * CAP], BF16, kind="ExternalOutput").ap()

    rg_bf = nc.dram_tensor("rg_bf", [DM, 4096], BF16, kind="Internal").ap()
    ru_bf = nc.dram_tensor("ru_bf", [DM, 4096], BF16, kind="Internal").ap()
    rd_bf = nc.dram_tensor("rd_bf", [4096, DM], BF16, kind="Internal").ap()
    slotmap = nc.dram_tensor("slotmap", [CAP, 16], F32, kind="Internal").ap()

    p = Prog()
    with ExitStack() as stk:
        def sb(name, shape, dt):
            return stk.enter_context(nc.sbuf_tensor(name, shape, dt))

        def ps(name, shape, dt):
            return stk.enter_context(nc.psum_tensor(name, shape, dt))

        ec = sb("ec", [128, 1], F32)
        lg = sb("lg", [128, 1024], F32)
        m1 = sb("m1", [128, 128], F32)
        m2 = sb("m2", [128, 128], F32)
        am1 = sb("am1", [128, 128], F32)
        am2 = sb("am2", [128, 128], F32)
        eq = sb("eq", [128, 1024], F32)
        sel9 = sb("sel9", [128, 1024], F32)
        lgm = sb("lgm", [128, 1024], F32)
        e2 = sb("e2", [128, 128], F32)
        rw1 = sb("rw1", [128, 128], F32)
        rw2 = sb("rw2", [128, 128], F32)
        we = sb("we", [128, 128], F32)
        sele = sb("sele", [128, 128], F32)
        incl = sb("incl", [128, 128], F32)
        rank = sb("rank", [128, 128], F32)
        valid = sb("valid", [128, 128], F32)
        srk = sb("srk", [128, 128], F32)
        srk32 = sb("srk32", [128, 128], U32)
        tokid = sb("tokid", [128, 128], F32)
        io8 = sb("io8", [128, 1024], F32)
        io8m = sb("io8m", [128, 1024], F32)
        ones128 = sb("ones128", [128, 128], F32)
        ones_r = sb("ones_r", [1, 128], F32)
        rsrow = sb("rsrow", [1, 128], F32)
        rsincl = sb("rsincl", [1, 128], F32)
        offc = sb("offc", [128, 1], F32)
        scat = sb("scat", [128, 128 * 16], F32)
        slotsb = sb("slotsb", [128, NGRP * 16], F32)
        tok_sl = sb("tok_sl", [128, NGRP], F32)
        w_sl = sb("w_sl", [128, NGRP], F32)
        tok32 = sb("tok32", [128, NGRP], U32)
        sidxf = sb("sidxf", [128, NGRP], F32)
        sidx32 = sb("sidx32", [128, NGRP], U32)
        trcol = sb("trcol", [128, 1], F32)
        cn99 = sb("cn99", [128, 1], F32)
        cn3e9 = sb("cn3e9", [128, 1], F32)
        sltmp = sb("sltmp", [128, NGRP], F32)
        xg = sb("xg", [128, DM], BF16)
        xgT = sb("xgT", [128, 8 * CAP], BF16)
        zero_sb = sb("zero_sb", [128, DM], F32)
        rgt = [sb(f"rgt{i}", [128, 8 * 128], BF16) for i in range(2)]
        rut = [sb(f"rut{i}", [128, 8 * 128], BF16) for i in range(2)]
        rdt = [sb(f"rdt{i}", [128, DM], BF16) for i in range(2)]
        gu_tb = sb("gu_tb", [128, 32 * 512], BF16)
        gtmp = sb("gtmp", [128, 512], F32)
        contrib = sb("contrib", [128, DM], F32)

        ptr = ps("ptr", [128, 128], BF16)
        ptc = ps("ptc", [128, 128], F32)
        pg = ps("pg", [128, 512], F32)
        pu = ps("pu", [128, 512], F32)
        pd = [ps(f"pd{i}", [128, DM], F32) for i in range(2)]

        regs = {}

        def init_regs(e):
            regs["cap"] = e.alloc_register(name="rcap")
            regs["tok"] = e.alloc_register(name="rtok")
            e.reg_mov(regs["cap"], CAP - 1)
            return e.reg_mov(regs["tok"], T - 1)

        with nc.Block() as block:
            d_regs = p.add("G", init_regs)
            ident, identf, d_id, d_idf = ident_setup(p, nc, sb)
            d_cg = p.add("G", lambda e: e.dma_start(out=rg_bf, in_=rg_e), dma=True)
            d_cu = p.add("G", lambda e: e.dma_start(out=ru_bf, in_=ru_e), dma=True)
            d_cd = p.add("G", lambda e: e.dma_start(out=rd_bf, in_=rd_e), dma=True)
            d_zs = p.add("V", lambda e: e.memset(zero_sb[:], 0.0))
            d_zero = []
            for i in range(129):
                d_zero.append(p.add("Q", (lambda ii: lambda e: e.dma_start(
                    out=routed[ii * 128:(ii + 1) * 128, :], in_=zero_sb[:, :]))(i), deps=[d_zs], dma=True))
            d_zslot = p.add("Q", lambda e: e.dma_start(
                out=slotmap.rearrange("(p r) c -> p (r c)", p=128), in_=zero_sb[:, 0:CAP * 16 // 128]), deps=[d_zs], dma=True)

            d_tk = p.add("G", lambda e: e.iota(tokid[:, :], pattern=[[1, 128]], base=0, channel_multiplier=128, allow_small_or_imprecise_dtypes=True))
            d_tr = p.add("G", lambda e: e.iota(trcol[:, :], pattern=[[1, 1]], base=TRASH, channel_multiplier=1, allow_small_or_imprecise_dtypes=True))
            d_io8 = p.add("G", lambda e: e.iota(io8[:, :], pattern=[[0, 128], [1, 8]], base=0, channel_multiplier=0, allow_small_or_imprecise_dtypes=True))
            d_io8m = p.add("V", lambda e: e.tensor_scalar(out=io8m[:, :], in0=io8[:, :], scalar1=-99.0, scalar2=None, op0=ALU.add), deps=[d_io8])
            d_or = p.add("V", lambda e: e.memset(ones_r[:], 1.0))
            d_o128 = p.add("V", lambda e: e.memset(ones128[:], 1.0))
            d_c99 = p.add("V", lambda e: e.memset(cn99[:], 99.0))
            d_c3e9 = p.add("V", lambda e: e.memset(cn3e9[:], 3.0e9))
            d_ec = p.add("Q", lambda e: e.dma_start(out=ec[:], in_=e_col), dma=True)

            d_lg = p.add("Q", lambda e: e.dma_start(out=lg[:, :], in_=logits_full.rearrange("(p c) o -> p (c o)", p=128)), dma=True)
            lg3 = lg[:, :].rearrange("p (c o) -> p c o", o=8)
            eq3 = eq[:, :].rearrange("p (c o) -> p c o", o=8)
            sel93 = sel9[:, :].rearrange("p (c o) -> p c o", o=8)
            lgm3 = lgm[:, :].rearrange("p (c o) -> p c o", o=8)
            io83 = io8[:, :].rearrange("p (c o) -> p c o", o=8)

            def b3(t):
                return t[:, :].rearrange("p (c o) -> p c o", o=1).broadcast_to((128, 128, 8))

            d_m1 = p.add("V", lambda e: e.tensor_reduce(out=m1[:, :], in_=lg3, axis=AX, op=ALU.max), deps=[d_lg])
            d_eq = p.add("V", lambda e: e.tensor_tensor(out=eq3, in0=lg3, in1=b3(m1), op=ALU.is_equal), deps=[d_m1])
            d_s9 = p.add("V", lambda e: e.tensor_tensor(out=sel93, in0=eq3, in1=io8m[:, :].rearrange("p (c o) -> p c o", o=8), op=ALU.mult), deps=[d_eq, d_io8m])
            d_s9 = p.add("V", lambda e: e.tensor_scalar(out=sel9[:, :], in0=sel9[:, :], scalar1=99.0, scalar2=None, op0=ALU.add), deps=[d_s9])
            d_a1 = p.add("V", lambda e: e.tensor_reduce(out=am1[:, :], in_=sel93, axis=AX, op=ALU.min), deps=[d_s9])
            d_q1 = p.add("V", lambda e: e.tensor_tensor(out=eq3, in0=io83, in1=b3(am1), op=ALU.is_equal), deps=[d_a1, d_io8])
            d_lm = p.add("V", lambda e: e.scalar_tensor_tensor(out=lgm3, in0=eq3, scalar=-1e30, in1=lg3, op0=ALU.mult, op1=ALU.add), deps=[d_q1, d_lg])
            d_m2 = p.add("V", lambda e: e.tensor_reduce(out=m2[:, :], in_=lgm3, axis=AX, op=ALU.max), deps=[d_lm])
            d_eq2 = p.add("V", lambda e: e.tensor_tensor(out=eq3, in0=lgm3, in1=b3(m2), op=ALU.is_equal), deps=[d_m2, d_lm])
            d_s92 = p.add("V", lambda e: e.tensor_tensor(out=sel93, in0=eq3, in1=io8m[:, :].rearrange("p (c o) -> p c o", o=8), op=ALU.mult), deps=[d_eq2, d_io8m])
            d_s92 = p.add("V", lambda e: e.tensor_scalar(out=sel9[:, :], in0=sel9[:, :], scalar1=99.0, scalar2=None, op0=ALU.add), deps=[d_s92])
            d_a2 = p.add("V", lambda e: e.tensor_reduce(out=am2[:, :], in_=sel93, axis=AX, op=ALU.min), deps=[d_s92])
            d_d21 = p.add("V", lambda e: e.tensor_tensor(out=e2[:, :], in0=m2[:, :], in1=m1[:, :], op=ALU.subtract), deps=[d_m2, d_m1])
            d_ex = p.add("S", lambda e: e.activation(out=e2[:, :], in_=e2[:, :], func=AF.Exp), deps=[d_d21])
            d_dn = p.add("V", lambda e: e.tensor_scalar(out=rw1[:, :], in0=e2[:, :], scalar1=1.0, scalar2=None, op0=ALU.add), deps=[d_ex])
            d_r1 = p.add("V", lambda e: e.reciprocal(out=rw1[:, :], in_=rw1[:, :]), deps=[d_dn])
            d_r2 = p.add("V", lambda e: e.tensor_tensor(out=rw2[:, :], in0=e2[:, :], in1=rw1[:, :], op=ALU.mult), deps=[d_r1, d_ex])
            d_e1 = p.add("V", lambda e: e.tensor_scalar(out=we[:, :], in0=am1[:, :], scalar1=ec[:, :], scalar2=None, op0=ALU.is_equal), deps=[d_a1, d_ec])
            d_w1 = p.add("V", lambda e: e.tensor_tensor(out=we[:, :], in0=we[:, :], in1=rw1[:, :], op=ALU.mult), deps=[d_e1, d_r1])
            d_e2m = p.add("V", lambda e: e.tensor_scalar(out=sele[:, :], in0=am2[:, :], scalar1=ec[:, :], scalar2=None, op0=ALU.is_equal), deps=[d_a2, d_ec])
            d_w2 = p.add("V", lambda e: e.tensor_tensor(out=sele[:, :], in0=sele[:, :], in1=rw2[:, :], op=ALU.mult), deps=[d_e2m, d_r2])
            d_we = p.add("V", lambda e: e.tensor_tensor(out=we[:, :], in0=we[:, :], in1=sele[:, :], op=ALU.add), deps=[d_w1, d_w2])
            d_se = p.add("V", lambda e: e.tensor_scalar(out=sele[:, :], in0=we[:, :], scalar1=0.0, scalar2=None, op0=ALU.is_gt), deps=[d_we])
            d_ic = p.add("V", lambda e: e.tensor_tensor_scan(out=incl[:, :], data0=ones128[:, :], data1=sele[:, :], initial=0.0, op0=ALU.mult, op1=ALU.add), deps=[d_se, d_o128])
            d_rx = p.add("V", lambda e: e.tensor_tensor(out=rank[:, :], in0=incl[:, :], in1=sele[:, :], op=ALU.subtract), deps=[d_ic])
            d_tp = p.add("T", lambda e: e.transpose(ptc[0:1, 0:128], incl[:, 127:128], identf[:, :]), deps=[d_ic, d_idf])
            d_rs = p.add("S", lambda e: e.copy(out=rsrow[:, :], in_=ptc[0:1, 0:128]), deps=[d_tp])
            d_ip = p.add("V", lambda e: e.tensor_tensor_scan(out=rsincl[:, :], data0=ones_r[:, :], data1=rsrow[:, :], initial=0.0, op0=ALU.mult, op1=ALU.add), deps=[d_rs, d_or])
            d_ep = p.add("V", lambda e: e.tensor_tensor(out=rsincl[:, :], in0=rsincl[:, :], in1=rsrow[:, :], op=ALU.subtract), deps=[d_ip])
            d_tb = p.add("T", lambda e: e.transpose(ptc[:, 0:1], rsincl[0:1, :], identf[0:1, 0:1]), deps=[d_ep, d_rs])
            d_of = p.add("S", lambda e: e.copy(out=offc[:, :], in_=ptc[:, 0:1]), deps=[d_tb])
            d_rk = p.add("V", lambda e: e.tensor_scalar(out=rank[:, :], in0=rank[:, :], scalar1=offc[:, :], scalar2=None, op0=ALU.add), deps=[d_rx, d_of])
            d_lt = p.add("V", lambda e: e.tensor_scalar(out=valid[:, :], in0=rank[:, :], scalar1=float(CAP), scalar2=None, op0=ALU.is_lt), deps=[d_rk])
            d_vd = p.add("V", lambda e: e.tensor_tensor(out=valid[:, :], in0=valid[:, :], in1=sele[:, :], op=ALU.mult), deps=[d_lt, d_se])
            d_sk = p.add("V", lambda e: e.tensor_scalar(out=srk[:, :], in0=rank[:, :], scalar1=-1.0e6, scalar2=None, op0=ALU.add), deps=[d_rk])
            d_sk = p.add("V", lambda e: e.tensor_tensor(out=srk[:, :], in0=srk[:, :], in1=valid[:, :], op=ALU.mult), deps=[d_sk, d_vd])
            d_sk = p.add("V", lambda e: e.tensor_scalar(out=srk[:, :], in0=srk[:, :], scalar1=1.0e6, scalar2=None, op0=ALU.add), deps=[d_sk])
            d_sk32 = p.add("V", lambda e: e.tensor_copy(srk32[:, :], srk[:, :]), deps=[d_sk])
            scat3 = scat[:, :].rearrange("p (c o) -> p c o", o=16)
            d_sc0 = p.add("V", lambda e: e.tensor_copy(scat3[:, :, 0:1], tokid[:, :].rearrange("p (c o) -> p c o", o=1)), deps=[d_tk])
            d_sc1 = p.add("V", lambda e: e.tensor_copy(scat3[:, :, 1:2], we[:, :].rearrange("p (c o) -> p c o", o=1)), deps=[d_we, d_sc0])
            d_scats = []
            for c in range(128):
                d_scats.append(p.add("G", (lambda cc: lambda e: e.indirect_dma_start(
                    out=slotmap, out_offset=bass.IndirectOffsetOnAxis(ap=srk32[:, cc:cc + 1], axis=0),
                    in_=scat[:, cc * 16:(cc + 1) * 16],
                    in_offset=None, bounds_check=regs["cap"], oob_is_err=False))(c),
                    deps=[d_sk32, d_sc1, d_zslot], dma=True))

            d_sl = p.add("Q", lambda e: e.dma_start(
                out=slotsb[:, :].rearrange("p (g o) -> p g o", o=16),
                in_=slotmap.rearrange("(g p) o -> p g o", p=128)), deps=d_scats, dma=True)
            slot3 = slotsb[:, :].rearrange("p (g o) -> p g o", o=16)
            d_ts = p.add("V", lambda e: e.tensor_copy(tok_sl[:, :].rearrange("p (g o) -> p g o", o=1), slot3[:, :, 0:1]), deps=[d_sl])
            d_ws = p.add("V", lambda e: e.tensor_copy(w_sl[:, :].rearrange("p (g o) -> p g o", o=1), slot3[:, :, 1:2]), deps=[d_sl])
            d_t32 = p.add("V", lambda e: e.tensor_copy(tok32[:, :], tok_sl[:, :]), deps=[d_ts])
            d_vg = p.add("V", lambda e: e.tensor_scalar(out=sltmp[:, :], in0=w_sl[:, :], scalar1=0.0, scalar2=None, op0=ALU.is_gt), deps=[d_ws])
            d_si = p.add("V", lambda e: e.tensor_tensor(out=sidxf[:, :], in0=tok_sl[:, :], in1=trcol[:, :].broadcast_to((128, NGRP)), op=ALU.subtract), deps=[d_ts, d_tr])
            d_si = p.add("V", lambda e: e.tensor_tensor(out=sidxf[:, :], in0=sidxf[:, :], in1=sltmp[:, :], op=ALU.mult), deps=[d_si, d_vg])
            d_si = p.add("V", lambda e: e.tensor_tensor(out=sidxf[:, :], in0=sidxf[:, :], in1=trcol[:, :].broadcast_to((128, NGRP)), op=ALU.add), deps=[d_si])
            d_s32 = p.add("V", lambda e: e.tensor_copy(sidx32[:, :], sidxf[:, :]), deps=[d_si])

            if debug:
                p.add("Q", lambda e: e.dma_start(out=dbg_rank, in_=rank[:, :]), deps=[d_rk], dma=True)
                p.add("Q", lambda e: e.dma_start(out=dbg_we, in_=we[:, :]), deps=[d_we], dma=True)
                p.add("Q", lambda e: e.dma_start(out=dbg_srk, in_=srk[:, :]), deps=[d_sk], dma=True)
                p.add("Q", lambda e: e.dma_start(out=dbg_tok, in_=tok_sl[:, :]), deps=[d_ts], dma=True)
                p.add("Q", lambda e: e.dma_start(out=dbg_wsl, in_=w_sl[:, :]), deps=[d_ws], dma=True)
                p.add("Q", lambda e: e.dma_start(out=dbg_am1, in_=am1[:, :]), deps=[d_a1], dma=True)
                p.add("Q", lambda e: e.dma_start(out=dbg_am2, in_=am2[:, :]), deps=[d_a2], dma=True)
            lastev = None
            for c in range(NGRP):
                d_g = p.add("G", (lambda cc: lambda e: e.indirect_dma_start(
                    out=xg[:, :], out_offset=None, in_=xf_full,
                    in_offset=bass.IndirectOffsetOnAxis(ap=tok32[:, cc:cc + 1], axis=0),
                    bounds_check=regs["tok"], oob_is_err=False))(c), deps=[d_t32, lastev], dma=True)
                for dmc in range(8):
                    d_t = p.add("T", (lambda dd: lambda e: e.transpose(
                        ptr[:, :], xg[:, dd * 128:(dd + 1) * 128], ident[:, :]))(dmc), deps=[d_g, d_id, lastev])
                    lastev = p.add("S", (lambda dd, cc: lambda e: e.copy(
                        out=xgT[:, dd * CAP + cc * 128: dd * CAP + (cc + 1) * 128], in_=ptr[:, :]))(dmc, c), deps=[d_t])
            d_xgT = lastev
            if debug:
                p.add("Q", lambda e: e.dma_start(out=dbg_xgT, in_=xgT[:, :]), deps=[d_xgT], dma=True)

            d_last_sc = None
            for tb in range(5):
                d_gu = {}
                for irc in range(32):
                    pp = irc % 2
                    war = [d_gu.get(irc - 2)]
                    d_lrg = p.add("Q", (lambda ii, b2: lambda e: e.dma_start(
                        out=rgt[b2][:, :].rearrange("p (c o) -> p c o", c=8),
                        in_=rg_bf[:, ii * 128:(ii + 1) * 128].rearrange("(c p) o -> p c o", p=128)))(irc, pp),
                        deps=[d_cg] + war, dma=True)
                    d_lru = p.add("G", (lambda ii, b2: lambda e: e.dma_start(
                        out=rut[b2][:, :].rearrange("p (c o) -> p c o", c=8),
                        in_=ru_bf[:, ii * 128:(ii + 1) * 128].rearrange("(c p) o -> p c o", p=128)))(irc, pp),
                        deps=[d_cu] + war, dma=True)
                    d_mg = d_mu = None
                    warp = [d_gu.get(irc - 1)]
                    for dmc in range(8):
                        d_mg = p.add("T", (lambda dd, bb, b2: lambda e: e.matmul(
                            pg[:, :], rgt[b2][:, dd * 128:(dd + 1) * 128],
                            xgT[:, dd * CAP + bb * 512:dd * CAP + (bb + 1) * 512],
                            start=(dd == 0), stop=(dd == 7)))(dmc, tb, pp), deps=[d_lrg, d_xgT] + (warp if dmc == 0 else []))
                        d_mu = p.add("T", (lambda dd, bb, b2: lambda e: e.matmul(
                            pu[:, :], rut[b2][:, dd * 128:(dd + 1) * 128],
                            xgT[:, dd * CAP + bb * 512:dd * CAP + (bb + 1) * 512],
                            start=(dd == 0), stop=(dd == 7)))(dmc, tb, pp), deps=[d_lru] + (warp if dmc == 0 else []))
                    d_sg = p.add("S", lambda e: e.activation(out=gtmp[:, :], in_=pg[:, :], func=AF.Silu), deps=[d_mg] + warp)
                    d_gu[irc] = p.add("V", (lambda ii: lambda e: e.tensor_tensor(
                        out=gu_tb[:, ii * 512:(ii + 1) * 512], in0=gtmp[:, :], in1=pu[:, :], op=ALU.mult))(irc), deps=[d_sg, d_mu])
                for ts2 in range(4):
                    cslot = tb * 4 + ts2
                    pdx = pd[ts2 % 2]
                    d_md = None
                    d_md_list = []
                    for irc in range(32):
                        pp = irc % 2
                        d_lrd = p.add("Q" if irc % 2 == 0 else "G", (lambda ii, b2: lambda e: e.dma_start(
                            out=rdt[b2][:, :], in_=rd_bf[ii * 128:(ii + 1) * 128, :]))(irc, pp),
                            deps=[d_cd, d_gu[irc], d_md_list[-2] if len(d_md_list) >= 2 else None, d_last_sc], dma=True)
                        for h in range(2):
                            d_md = p.add("T", (lambda ii, hh, t2, b2: lambda e: e.matmul(
                                pdx[:, hh * 512:(hh + 1) * 512],
                                gu_tb[:, ii * 512 + t2 * 128: ii * 512 + (t2 + 1) * 128],
                                rdt[b2][:, hh * 512:(hh + 1) * 512],
                                start=(ii == 0), stop=(ii == 31)))(irc, h, ts2, pp), deps=[d_lrd])
                        d_md_list.append(d_md)
                    d_ct = p.add("V", (lambda cc: lambda e: e.tensor_scalar(
                        out=contrib[:, :], in0=pdx[:, :], scalar1=w_sl[:, cc:cc + 1], scalar2=None, op0=ALU.mult))(cslot),
                        deps=[d_md, d_ws, d_last_sc])
                    d_last_sc = p.add("G", (lambda cc: lambda e: e.indirect_dma_start(
                        out=routed, out_offset=bass.IndirectOffsetOnAxis(ap=sidx32[:, cc:cc + 1], axis=0),
                        in_=contrib[:, :], in_offset=None, oob_is_err=True, compute_op=ALU.add))(cslot),
                        deps=[d_ct, d_s32] + d_zero, dma=True)

            p.add("V", lambda e: e.engine_nop(), deps=[d_last_sc])
            p.build(nc, block)
    return nc


def build_L5():
    nc = bass.Bass("TRN2", target_bir_lowering=False)
    xmid = nc.dram_tensor("xmid", [TL, DM], F32, kind="ExternalInput").ap()
    xf_bf = nc.dram_tensor("xf_bf", [TL, DM], BF16, kind="ExternalInput").ap()
    routed8 = nc.dram_tensor("routed8", [8, TL, DM], F32, kind="ExternalInput").ap()
    sg_w = nc.dram_tensor("sg_w", [DM, L], F32, kind="ExternalInput").ap()
    su_w = nc.dram_tensor("su_w", [DM, L], F32, kind="ExternalInput").ap()
    sd_w = nc.dram_tensor("sd_w", [L, DM], F32, kind="ExternalInput").ap()

    out = nc.dram_tensor("out", [TL, DM], F32, kind="ExternalOutput").ap()

    sg_bf = nc.dram_tensor("sg_bf", [DM, L], BF16, kind="Internal").ap()
    su_bf = nc.dram_tensor("su_bf", [DM, L], BF16, kind="Internal").ap()
    sd_bf = nc.dram_tensor("sd_bf", [L, DM], BF16, kind="Internal").ap()

    p = Prog()
    with ExitStack() as stk:
        def sb(name, shape, dt):
            return stk.enter_context(nc.sbuf_tensor(name, shape, dt))

        def ps(name, shape, dt):
            return stk.enter_context(nc.psum_tensor(name, shape, dt))

        xfb = sb("xfb", [128, DM], BF16)
        xfT = sb("xfT", [128, 8 * TL], BF16)
        sgt = [sb(f"sgt{i}", [128, 8 * 128], BF16) for i in range(2)]
        sut = [sb(f"sut{i}", [128, 8 * 128], BF16) for i in range(2)]
        sdt = [sb(f"sdt{i}", [128, DM], BF16) for i in range(2)]
        gu_tb = sb("gu_tb", [128, 64 * 512], BF16)
        gtmp = sb("gtmp", [128, 512], F32)
        acc = sb("acc", [128, DM], F32)
        r1t = sb("r1t", [128, DM], F32)
        xmt = sb("xmt", [128, DM], F32)
        lnt = sb("lnt", [128, DM], F32)
        sumt = sb("sumt", [128, 1], F32)
        sqt = sb("sqt", [128, DM], F32)
        ssqt = sb("ssqt", [128, 1], F32)
        meant = sb("meant", [128, 1], F32)
        vart = sb("vart", [128, 1], F32)
        rstdt = sb("rstdt", [128, 1], F32)

        ptr = ps("ptr", [128, 128], BF16)
        pg = ps("pg", [128, 512], F32)
        pu = ps("pu", [128, 512], F32)
        pd = [ps(f"pd{i}", [128, DM], F32) for i in range(2)]

        with nc.Block() as block:
            ident, identf, d_id, d_idf = ident_setup(p, nc, sb)
            csts, d_cst = const_cols(p, sb, [LNEPS])
            d_csg = p.add("G", lambda e: e.dma_start(out=sg_bf, in_=sg_w), dma=True)
            d_csu = p.add("G", lambda e: e.dma_start(out=su_bf, in_=su_w), dma=True)
            d_csd = p.add("G", lambda e: e.dma_start(out=sd_bf, in_=sd_w), dma=True)

            lastev = None
            for i in range(16):
                d_lx = p.add("Q", (lambda ii: lambda e: e.dma_start(out=xfb[:, :], in_=xf_bf[ii * 128:(ii + 1) * 128, :]))(i),
                             deps=[lastev], dma=True)
                for c in range(8):
                    d_t = p.add("T", (lambda cc: lambda e: e.transpose(
                        ptr[:, :], xfb[:, cc * 128:(cc + 1) * 128], ident[:, :]))(c), deps=[d_lx, d_id, lastev])
                    lastev = p.add("S", (lambda cc, ii: lambda e: e.copy(
                        out=xfT[:, cc * TL + ii * 128: cc * TL + (ii + 1) * 128], in_=ptr[:, :]))(c, i), deps=[d_t])
            d_xT = lastev

            lnscr = (sumt, sqt, ssqt, vart, meant, rstdt)
            d_prev_out = None
            for tb in range(4):
                d_gu = {}
                for ihc in range(64):
                    pp = ihc % 2
                    war = [d_gu.get(ihc - 2)]
                    d_lg = p.add("Q", (lambda ii, b2: lambda e: e.dma_start(
                        out=sgt[b2][:, :].rearrange("p (c o) -> p c o", c=8),
                        in_=sg_bf[:, ii * 128:(ii + 1) * 128].rearrange("(c p) o -> p c o", p=128)))(ihc, pp),
                        deps=[d_csg] + war, dma=True)
                    d_lu = p.add("G", (lambda ii, b2: lambda e: e.dma_start(
                        out=sut[b2][:, :].rearrange("p (c o) -> p c o", c=8),
                        in_=su_bf[:, ii * 128:(ii + 1) * 128].rearrange("(c p) o -> p c o", p=128)))(ihc, pp),
                        deps=[d_csu] + war, dma=True)
                    d_mg = d_mu = None
                    warp = [d_gu.get(ihc - 1)]
                    for dmc in range(8):
                        d_mg = p.add("T", (lambda dd, bb, b2: lambda e: e.matmul(
                            pg[:, :], sgt[b2][:, dd * 128:(dd + 1) * 128],
                            xfT[:, dd * TL + bb * 512: dd * TL + (bb + 1) * 512],
                            start=(dd == 0), stop=(dd == 7)))(dmc, tb, pp), deps=[d_lg, d_xT] + (warp if dmc == 0 else []))
                        d_mu = p.add("T", (lambda dd, bb, b2: lambda e: e.matmul(
                            pu[:, :], sut[b2][:, dd * 128:(dd + 1) * 128],
                            xfT[:, dd * TL + bb * 512: dd * TL + (bb + 1) * 512],
                            start=(dd == 0), stop=(dd == 7)))(dmc, tb, pp), deps=[d_lu] + (warp if dmc == 0 else []))
                    d_sg2 = p.add("S", lambda e: e.activation(out=gtmp[:, :], in_=pg[:, :], func=AF.Silu), deps=[d_mg] + warp)
                    d_gu[ihc] = p.add("V", (lambda ii: lambda e: e.tensor_tensor(
                        out=gu_tb[:, ii * 512:(ii + 1) * 512], in0=gtmp[:, :], in1=pu[:, :], op=ALU.mult))(ihc), deps=[d_sg2, d_mu])
                for ts2 in range(4):
                    ti = tb * 4 + ts2
                    pdx = pd[ts2 % 2]
                    d_md = None
                    d_md_list = []
                    for ihc in range(64):
                        pp = ihc % 2
                        d_lrd = p.add("Q" if ihc % 2 == 0 else "G", (lambda ii, b2: lambda e: e.dma_start(
                            out=sdt[b2][:, :], in_=sd_bf[ii * 128:(ii + 1) * 128, :]))(ihc, pp),
                            deps=[d_csd, d_gu[ihc], d_md_list[-2] if len(d_md_list) >= 2 else None, d_prev_out], dma=True)
                        for h in range(2):
                            d_md = p.add("T", (lambda ii, hh, t2, b2: lambda e: e.matmul(
                                pdx[:, hh * 512:(hh + 1) * 512],
                                gu_tb[:, ii * 512 + t2 * 128: ii * 512 + (t2 + 1) * 128],
                                sdt[b2][:, hh * 512:(hh + 1) * 512],
                                start=(ii == 0), stop=(ii == 63)))(ihc, h, ts2, pp), deps=[d_lrd])
                        d_md_list.append(d_md)
                    d_ac = p.add("V", lambda e: e.tensor_copy(acc[:, :], pdx[:, :]), deps=[d_md])
                    for ee in range(8):
                        d_lr = p.add("Q" if ee % 2 == 0 else "G", (lambda eee, t2: lambda e: e.dma_start(
                            out=r1t[:, :], in_=routed8[eee, t2 * 128:(t2 + 1) * 128, :]))(ee, ti),
                            deps=[d_ac], dma=True)
                        d_ac = p.add("V", lambda e: e.tensor_tensor(out=acc[:, :], in0=acc[:, :], in1=r1t[:, :], op=ALU.add), deps=[d_lr])
                    d_lm = p.add("Q", (lambda t2: lambda e: e.dma_start(out=xmt[:, :], in_=xmid[t2 * 128:(t2 + 1) * 128, :]))(ti),
                                 deps=[d_prev_out], dma=True)
                    d_ln = ln_tile(p, acc[:, :], lnscr, lnt[:, :], [d_ac, d_cst], csts[LNEPS])
                    d_fo = p.add("V", lambda e: e.tensor_tensor(out=xmt[:, :], in0=xmt[:, :], in1=lnt[:, :], op=ALU.add), deps=[d_ln, d_lm])
                    d_prev_out = p.add("Q", (lambda t2: lambda e: e.dma_start(
                        out=out[t2 * 128:(t2 + 1) * 128, :], in_=xmt[:, :]))(ti), deps=[d_fo], dma=True)

            p.add("V", lambda e: e.engine_nop(), deps=[d_prev_out])
            p.build(nc, block)
    return nc


_NC_CACHE = {}


def _get(name, builder):
    if name not in _NC_CACHE:
        _NC_CACHE[name] = builder()
    return _NC_CACHE[name]


def kernel(**inp):
    x = np.asarray(inp["L0_attn_hidden_states"])
    xf_flat = np.ascontiguousarray(x.reshape(T, DM))
    w_inT = np.ascontiguousarray(np.asarray(inp["L0_attn_in_proj_w"]).T)
    convwT = np.ascontiguousarray(np.asarray(inp["L0_attn_conv_w"])[:, 0, :].T)
    conv_b = np.ascontiguousarray(np.asarray(inp["L0_attn_conv_b"]).reshape(1, DI))
    xprojT = np.ascontiguousarray(np.asarray(inp["L0_attn_x_proj_w"]).T)
    dtwT = np.ascontiguousarray(np.asarray(inp["L0_attn_dt_proj_w"]).T)
    dtb = np.ascontiguousarray(np.asarray(inp["L0_attn_dt_proj_b"]).reshape(1, DI))
    alog = np.ascontiguousarray(np.asarray(inp["L0_attn_A_log"]))
    Dv = np.ascontiguousarray(np.asarray(inp["L0_attn_D"]).reshape(DI, 1))
    woutT = np.ascontiguousarray(np.asarray(inp["L0_attn_out_proj_w"]).T)
    rwT = np.ascontiguousarray(np.asarray(inp["L0_ffn_router_w"]).T)
    rg = np.asarray(inp["L0_ffn_rg"])
    ru = np.asarray(inp["L0_ffn_ru"])
    rd = np.asarray(inp["L0_ffn_rd"])
    sg = np.ascontiguousarray(np.asarray(inp["L0_ffn_sg"]))
    su = np.ascontiguousarray(np.asarray(inp["L0_ffn_su"]))
    sd = np.ascontiguousarray(np.asarray(inp["L0_ffn_sd"]))

    cores = list(range(NCORES))

    in1 = []
    for c in cores:
        lo = c * TL
        halo = np.zeros((3, DM), np.float32) if c % 4 == 0 else xf_flat[lo - 3:lo]
        xh = np.ascontiguousarray(np.concatenate(
            [halo, xf_flat[lo:lo + TL], np.zeros((125, DM), np.float32)], 0))
        in1.append(dict(x_halo=xh, w_inT=w_inT, convwT=convwT, conv_b=conv_b,
                        xprojT=xprojT, dtwT=dtwT, dtb=dtb))
    r1 = run_bass_kernel_spmd(_get("L1", build_L1), in1, core_ids=cores).results

    in2 = []
    for c2 in cores:
        b = c2 // 4
        dq = c2 % 4
        cs = [4 * b + j for j in range(4)]
        dfm = np.ascontiguousarray(
            np.concatenate([r1[j]["delta_out"] for j in cs], 0).T[dq * DSL:(dq + 1) * DSL])
        ufm = np.ascontiguousarray(
            np.concatenate([r1[j]["u_out"] for j in cs], 0).T[dq * DSL:(dq + 1) * DSL])
        zfm = np.ascontiguousarray(
            np.concatenate([r1[j]["z_buf"][3:3 + TL] for j in cs], 0).T[dq * DSL:(dq + 1) * DSL])
        bct = np.ascontiguousarray(np.concatenate([r1[j]["bc_out"] for j in cs], 1))
        in2.append(dict(dfm=dfm, ufm=ufm, zfm=zfm, bct=bct,
                        alog=np.ascontiguousarray(alog[dq * DSL:(dq + 1) * DSL]),
                        dsl=np.ascontiguousarray(Dv[dq * DSL:(dq + 1) * DSL]),
                        woutT=np.ascontiguousarray(woutT[dq * DSL:(dq + 1) * DSL])))
    r2 = run_bass_kernel_spmd(_get("L2", build_L2), in2, core_ids=cores).results

    in3 = []
    for c in cores:
        b = c // 4
        lq = c % 4
        attn4 = np.ascontiguousarray(np.stack(
            [r2[4 * b + dq]["attn_part"][lq * TL:(lq + 1) * TL] for dq in range(4)], 0))
        in3.append(dict(attn4=attn4, x_loc=np.ascontiguousarray(xf_flat[c * TL:(c + 1) * TL]), rwT=rwT))
    r3 = run_bass_kernel_spmd(_get("L3", build_L3), in3, core_ids=cores).results

    xf_full = np.ascontiguousarray(np.concatenate([r3[c]["xf_out"] for c in cores], 0))
    logits_full = np.ascontiguousarray(
        np.concatenate([r3[c]["logitsT_out"] for c in cores], 1).T.astype(np.float32))
    in4 = []
    for c in cores:
        in4.append(dict(xf_full=xf_full, logits_full=logits_full,
                        e_col=np.full((128, 1), float(c), np.float32),
                        rg_e=np.ascontiguousarray(rg[c]),
                        ru_e=np.ascontiguousarray(ru[c]),
                        rd_e=np.ascontiguousarray(rd[c])))
    r4 = run_bass_kernel_spmd(_get("L4", build_L4), in4, core_ids=cores).results

    in5 = []
    for c in cores:
        routed8 = np.ascontiguousarray(np.stack(
            [r4[e]["routed"][c * TL:(c + 1) * TL] for e in cores], 0))
        in5.append(dict(xmid=np.ascontiguousarray(r3[c]["xmid_out"]),
                        xf_bf=np.ascontiguousarray(r3[c]["xf_out"]),
                        routed8=routed8, sg_w=sg, su_w=su, sd_w=sd))
    r5 = run_bass_kernel_spmd(_get("L5", build_L5), in5, core_ids=cores).results

    out = np.concatenate([r5[c]["out"] for c in cores], 0)
    return np.ascontiguousarray(out.reshape(2, L, DM).astype(np.float32))


# version 03 (final)
# speedup vs baseline: 1.0236x; 1.0236x over previous
import numpy as np
from contextlib import ExitStack
import concourse.bass as bass
import concourse.mybir as mybir
from concourse.bass_utils import run_bass_kernel_spmd

F32 = mybir.dt.float32
BF16 = mybir.dt.bfloat16
U32 = mybir.dt.uint32
AF = mybir.ActivationFunctionType
ALU = mybir.AluOpType
AX = mybir.AxisListType.X

NCORES = 8
TL = 2048
T = 16384
DM = 1024
DI = 2048
DTR = 64
NS = 16
L = 8192
DSL = 512
CAP = 2560
NGRP = CAP // 128
TRASH = T
LNEPS = 1e-5
RMSEPS = 1.1920929e-07

ENGS = ("T", "V", "S", "G", "Q")


class Prog:

    def __init__(self):
        self.steps = []
        self.keys = [(e, k) for e in ENGS for k in ("c", "d")]
        self.count = {k: 0 for k in self.keys}

    def add(self, eng, fn, deps=(), dma=False):
        key = (eng, "d" if dma else "c")
        inc = 16 if dma else 1
        deps = [d for d in deps if d is not None]
        if dma and self.count[key] > 0:
            deps.append((key, self.count[key]))
        self.count[key] += inc
        self.steps.append((eng, key, fn, tuple(deps), inc))
        return (key, self.count[key])

    def reset_gdma(self, deps=()):
        deps = [d for d in deps if d is not None]
        if self.count[("G", "d")] > 0:
            deps.append((("G", "d"), self.count[("G", "d")]))
        tok = self.add("G", lambda e: e.dma_reset(), deps=deps)
        self.count[("G", "d")] = 0
        return tok

    def build(self, nc, block):
        per_eng = {e: [] for e in ENGS}
        for eng, key, fn, deps, inc in self.steps:
            per_eng[eng].append((key, fn, deps, inc))
        sems = {k: nc.semaphore(f"sem_{k[0]}{k[1]}").__enter__() for k in self.keys}

        def make_section(e):
            def section(engine):
                waited = {k: 0 for k in self.keys}
                for key, fn, deps, inc in per_eng[e]:
                    for (dk, dv) in deps:
                        if dv > waited[dk]:
                            engine.wait_ge(sems[dk], dv)
                            waited[dk] = dv
                    ins = fn(engine)
                    ins.then_inc(sems[key], inc)
            return section

        for e, attr in (("T", "tensor"), ("V", "vector"), ("S", "scalar"),
                        ("G", "gpsimd"), ("Q", "sync")):
            getattr(block, attr)(make_section(e))


def ln_tile(p, xt, scratch, out_ap, deps, epsc, nf=DM, pr=128):
    sumt, sqt, ssqt, vart, meant, rstdt = scratch
    sumt = sumt[0:pr, :]; sqt = sqt[0:pr, 0:nf]; ssqt = ssqt[0:pr, :]
    vart = vart[0:pr, :]; meant = meant[0:pr, :]; rstdt = rstdt[0:pr, :]
    epsc_l = None
    d1 = p.add("V", lambda e: e.tensor_reduce(out=sumt, in_=xt, axis=AX, op=ALU.add), deps=deps)
    d2 = p.add("S", lambda e: e.activation(out=sqt, in_=xt, func=AF.Square), deps=deps)
    d3 = p.add("V", lambda e: e.tensor_reduce(out=ssqt, in_=sqt, axis=AX, op=ALU.add), deps=[d2])
    d4 = p.add("V", lambda e: e.tensor_scalar(out=meant, in0=sumt, scalar1=1.0 / nf, scalar2=None, op0=ALU.mult), deps=[d1])
    d5 = p.add("S", lambda e: e.activation(out=vart, in_=meant, func=AF.Square), deps=[d4])
    d6 = p.add("V", lambda e: e.scalar_tensor_tensor(out=vart, in0=ssqt, scalar=1.0 / nf, in1=vart, op0=ALU.mult, op1=ALU.subtract), deps=[d3, d5])
    d7 = p.add("S", lambda e: e.activation(out=rstdt, in_=vart, func=AF.Sqrt, bias=epsc[0:pr, :]), deps=[d6])
    d8 = p.add("V", lambda e: e.reciprocal(out=rstdt, in_=rstdt), deps=[d7])
    d9 = p.add("V", lambda e: e.tensor_scalar(out=out_ap, in0=xt, scalar1=meant, scalar2=rstdt, op0=ALU.subtract, op1=ALU.mult), deps=[d8] + list(deps))
    return d9


def const_cols(p, sb, values, name="cst"):
    t = sb(name, [128, len(values)], F32)
    deps = []
    aps = {}
    for i, v in enumerate(values):
        deps.append(p.add("V", (lambda i2, v2: lambda e: e.memset(t[:, i2:i2 + 1], v2))(i, v)))
        aps[v] = t[:, i:i + 1]
    return aps, deps[-1]


def ident_setup(p, nc, sb):
    iot = sb("iot_id", [128, 128], F32)
    identf = sb("identf", [128, 128], F32)
    ident = sb("ident", [128, 128], BF16)
    d_io = p.add("G", lambda e: e.iota(iot[:, :], pattern=[[1, 128]], base=0, channel_multiplier=-1, allow_small_or_imprecise_dtypes=True))
    d_idf = p.add("V", lambda e: e.tensor_scalar(out=identf[:, :], in0=iot[:, :], scalar1=0.0, scalar2=None, op0=ALU.is_equal), deps=[d_io])
    d_id = p.add("V", lambda e: e.tensor_copy(ident[:, :], identf[:, :]), deps=[d_idf])
    return ident, identf, d_id, d_idf


def build_L1():
    nc = bass.Bass("TRN2", target_bir_lowering=False)
    NT = TL + 128
    x_halo = nc.dram_tensor("x_halo", [NT, DM], F32, kind="ExternalInput").ap()
    w_inT = nc.dram_tensor("w_inT", [DM, 2 * DI], F32, kind="ExternalInput").ap()
    convwT = nc.dram_tensor("convwT", [4, DI], F32, kind="ExternalInput").ap()
    conv_b = nc.dram_tensor("conv_b", [1, DI], F32, kind="ExternalInput").ap()
    xprojT = nc.dram_tensor("xprojT", [DI, 96], F32, kind="ExternalInput").ap()
    dtwT = nc.dram_tensor("dtwT", [DTR, DI], F32, kind="ExternalInput").ap()
    dtb = nc.dram_tensor("dtb", [1, DI], F32, kind="ExternalInput").ap()

    u_out = nc.dram_tensor("u_out", [TL, DI], F32, kind="ExternalOutput").ap()
    z_buf = nc.dram_tensor("z_buf", [NT, DI], F32, kind="ExternalOutput").ap()
    delta_out = nc.dram_tensor("delta_out", [TL, DI], F32, kind="ExternalOutput").ap()
    bc_out = nc.dram_tensor("bc_out", [32, TL], F32, kind="ExternalOutput").ap()

    w_inT_bf = nc.dram_tensor("w_inT_bf", [DM, 2 * DI], BF16, kind="Internal").ap()
    xpre_d = nc.dram_tensor("xpre_d", [NT, DI], F32, kind="Internal").ap()

    p = Prog()
    with ExitStack() as stk:
        def sb(name, shape, dt):
            return stk.enter_context(nc.sbuf_tensor(name, shape, dt))

        def ps(name, shape, dt):
            return stk.enter_context(nc.psum_tensor(name, shape, dt))

        ones1 = sb("ones1", [1, 128], F32)
        wsb = sb("wsb", [128, 8 * 4096], BF16)
        cw = sb("cw", [128, 4 * DI], F32)
        cb = sb("cb", [128, DI], F32)
        db2 = sb("db2", [128, DI], F32)
        rowbuf = sb("rowbuf", [1, DI], F32)
        xprojsb = sb("xprojsb", [128, 16 * 96], BF16)
        dtwsb = sb("dtwsb", [DTR, DI], BF16)
        xt = sb("xt", [128, DM], F32)
        sumt = sb("sumt", [128, 1], F32)
        ssqt = sb("ssqt", [128, 1], F32)
        meant = sb("meant", [128, 1], F32)
        vart = sb("vart", [128, 1], F32)
        rstdt = sb("rstdt", [128, 1], F32)
        lnbf = sb("lnbf", [128, DM], BF16)
        lnT = sb("lnT", [128, DM], BF16)
        xz = sb("xz", [128, DI], F32)
        cx = [sb("cx0", [128, DI], F32)] * 2
        cacc = sb("cacc", [128, DI], F32)
        ut = sb("ut", [128, DI], F32)
        ubf = sb("ubf", [128, DI], BF16)
        uT = sb("uT", [128, DI], BF16)
        xdbl = sb("xdbl", [96, 128], F32)
        xdblbf = sb("xdblbf", [96, 128], BF16)
        vt = sb("vt", [128, DI], F32)
        s1 = sb("s1", [128, DI], F32)
        s2 = sb("s2", [128, DI], F32)
        sqt = s1
        ctmp = vt

        ptr = ps("ptr", [128, 128], BF16)
        pmm = ps("pmm", [128, 2048], F32)
        pxd = ps("pxd", [96, 128], F32)
        pcw = ps("pcw", [128, 512], F32)

        ident, identf, d_id, d_idf = None, None, None, None
        with nc.Block() as block:
            ident, identf, d_id, d_idf = ident_setup(p, nc, sb)
            d_ones = p.add("V", lambda e: e.memset(ones1[:], 1.0))
            csts, d_cst = const_cols(p, sb, [LNEPS, 1.0, -1.0, 2.0])

            d_wc = p.add("G", lambda e: e.dma_start(out=w_inT_bf, in_=w_inT), dma=True)
            d_dtw = p.add("G", lambda e: e.dma_start(out=dtwsb[:], in_=dtwT), dma=True)
            d_xp = [p.add("G", (lambda c: lambda e: e.dma_start(
                out=xprojsb[:, c * 96:(c + 1) * 96], in_=xprojT[c * 128:(c + 1) * 128, :]))(ch), dma=True) for ch in range(16) for c in [ch]]
            d_w = [p.add("G", (lambda c: lambda e: e.dma_start(
                out=wsb[:, c * 4096:(c + 1) * 4096], in_=w_inT_bf[c * 128:(c + 1) * 128, :]))(ch), deps=[d_wc], dma=True) for ch in range(8) for c in [ch]]

            last_evac = None
            d_row = None
            jobs = [(convwT[j:j + 1, :], cw, j * DI, 1.0) for j in range(4)]
            jobs.append((conv_b, cb, 0, 1.0))
            jobs.append((dtb, db2, 0, 2.0))
            for src_ap, dst, off, scale in jobs:
                d_row = p.add("Q", (lambda sa: lambda e: e.dma_start(out=rowbuf[:], in_=sa))(src_ap),
                              deps=[last_evac, d_cst], dma=True)
                for f4 in range(4):
                    d_mm = p.add("T", (lambda ff: lambda e: e.matmul(
                        pcw[:, :], ones1[:, :], rowbuf[0:1, ff * 512:(ff + 1) * 512],
                        start=True, stop=True))(f4), deps=[d_row, d_ones, last_evac])
                    last_evac = p.add("S", (lambda dd, oo, ff, ss: lambda e: e.activation(
                        out=dd[:, oo + ff * 512: oo + (ff + 1) * 512], in_=pcw[:, :], func=AF.Copy,
                        scale=csts[ss]))(dst, off, f4, scale), deps=[d_mm])
            d_setup = last_evac

            lnscr = (sumt, sqt, ssqt, vart, meant, rstdt)
            d_lastz = None
            d_lastx = None
            d_prev_dt = None
            for i in range(17):
                r0 = i * 128
                r1 = r0 + 128
                pr = 128
                d_x = p.add("Q", (lambda a, b: lambda e: e.dma_start(out=xt[0:b - a, :], in_=x_halo[a:b, :]))(r0, r1),
                            deps=[d_setup, d_lastz, d_lastx], dma=True)
                d_ln = ln_tile(p, xt[0:pr, :], lnscr, lnbf[0:pr, :], [d_x, d_cst], csts[LNEPS], pr=pr)
                d_tr = []
                lastev = None
                for c in range(8):
                    d_t = p.add("T", (lambda cc, pp: lambda e: e.transpose(
                        ptr[:, 0:pp], lnbf[0:pp, cc * 128:(cc + 1) * 128], ident[0:pp, 0:pp]))(c, pr), deps=[d_ln, d_id, lastev])
                    lastev = p.add("S", (lambda cc, pp: lambda e: e.copy(
                        out=lnT[:, cc * 128:cc * 128 + pp], in_=ptr[:, 0:pp]))(c, pr), deps=[d_t])
                    d_tr.append(lastev)
                d_ev = None
                for grp in range(2):
                    d_mm = None
                    for c in range(8):
                        for f in range(4):
                            d_mm = p.add("T", (lambda cc, ff, gg, pp: lambda e: e.matmul(
                                pmm[0:pp, ff * 512:(ff + 1) * 512],
                                lnT[:, cc * 128:cc * 128 + pp],
                                wsb[:, cc * 4096 + gg * 2048 + ff * 512: cc * 4096 + gg * 2048 + (ff + 1) * 512],
                                start=(cc == 0), stop=(cc == 7)))(c, f, grp, pr),
                                deps=[d_tr[c], d_ev, d_prev_dt] + (d_w if i == 0 and c == 0 else []))
                    d_ev = p.add("S", (lambda gg, pp: lambda e: e.copy(
                        out=xz[0:pp, :], in_=pmm[0:pp, :]))(grp, pr),
                        deps=[d_mm] + ([d_lastx] if grp == 1 else [d_lastz]))
                    if grp == 0:
                        d_lastx = p.add("Q", (lambda a, b: lambda e: e.dma_start(
                            out=xpre_d[a:b, :], in_=xz[0:b - a, :]))(r0, r1), deps=[d_ev], dma=True)
                    else:
                        d_lastz = p.add("Q", (lambda a, b: lambda e: e.dma_start(
                            out=z_buf[a:b, :], in_=xz[0:b - a, :]))(r0, r1), deps=[d_ev], dma=True)
                d_prev_dt = d_ev
            d_phaseA = [d_lastz, d_lastx]

            d_prev_out = None
            d_uo = None
            d_bco = None
            for i in range(16):
                deps0 = d_phaseA + [d_prev_out]
                d_cx = {}
                d_conv = None
                for j in range(4):
                    buf = cx[j % 2]
                    war = [d_conv] if j >= 1 else []
                    d_cx[j] = p.add("Q" if j % 2 == 0 else "G",
                                    (lambda ii, jj, bb: lambda e: e.dma_start(
                                        out=bb[:, :], in_=xpre_d[ii * 128 + jj: ii * 128 + jj + 128, :]))(i, j, buf),
                                    deps=deps0 + war, dma=True)
                    if j == 0:
                        d_conv = p.add("V", (lambda bb: lambda e: e.tensor_tensor(
                            out=cacc[:, :], in0=bb[:, :], in1=cw[:, 0:DI], op=ALU.mult))(buf), deps=[d_cx[0], d_setup])
                    else:
                        d_m = p.add("V", (lambda bb, jj: lambda e: e.tensor_tensor(
                            out=ctmp[:, :], in0=bb[:, :], in1=cw[:, jj * DI:(jj + 1) * DI], op=ALU.mult))(buf, j), deps=[d_cx[j], d_conv])
                        d_conv = p.add("V", lambda e: e.tensor_tensor(out=cacc[:, :], in0=cacc[:, :], in1=ctmp[:, :], op=ALU.add), deps=[d_m])
                d_conv = p.add("V", lambda e: e.tensor_tensor(out=cacc[:, :], in0=cacc[:, :], in1=cb[:, :], op=ALU.add), deps=[d_conv])
                d_u = p.add("S", lambda e: e.activation(out=ut[:, :], in_=cacc[:, :], func=AF.Silu), deps=[d_conv])
                d_uo = p.add("Q", (lambda ii: lambda e: e.dma_start(
                    out=u_out[ii * 128:(ii + 1) * 128, :], in_=ut[:, :]))(i), deps=[d_u], dma=True)
                d_ub = p.add("V", lambda e: e.tensor_copy(ubf[:, :], ut[:, :]), deps=[d_u])
                lastev = None
                d_trs = []
                for c in range(16):
                    d_t = p.add("T", (lambda cc: lambda e: e.transpose(
                        ptr[:, :], ubf[:, cc * 128:(cc + 1) * 128], ident[:, :]))(c), deps=[d_ub, d_id, lastev])
                    lastev = p.add("S", (lambda cc: lambda e: e.copy(
                        out=uT[:, cc * 128:(cc + 1) * 128], in_=ptr[:, :]))(c), deps=[d_t])
                    d_trs.append(lastev)
                d_xd = None
                for c in range(16):
                    d_xd = p.add("T", (lambda cc: lambda e: e.matmul(
                        pxd[:, :], xprojsb[:, cc * 96:(cc + 1) * 96], uT[:, cc * 128:(cc + 1) * 128],
                        start=(cc == 0), stop=(cc == 15)))(c), deps=[d_trs[c]] + (d_xp if i == 0 and c == 0 else []))
                d_xe = p.add("S", lambda e: e.copy(out=xdbl[:, :], in_=pxd[:, :]), deps=[d_xd])
                d_bco = p.add("Q", (lambda ii: lambda e: e.dma_start(
                    out=bc_out[:, ii * 128:(ii + 1) * 128], in_=xdbl[64:96, :]))(i), deps=[d_xe], dma=True)
                d_xb = p.add("V", lambda e: e.tensor_copy(xdblbf[0:64, :], xdbl[0:64, :]), deps=[d_xe])
                d_dt = None
                for f in range(4):
                    d_dt = p.add("T", (lambda ff: lambda e: e.matmul(
                        pmm[:, ff * 512:(ff + 1) * 512], xdblbf[0:64, :], dtwsb[:, ff * 512:(ff + 1) * 512],
                        start=True, stop=True))(f), deps=[d_xb, d_dtw, d_prev_out])
                d_v = p.add("V", lambda e: e.tensor_tensor(out=vt[:, :], in0=pmm[:, :], in1=db2[:, :], op=ALU.add), deps=[d_dt, d_setup])
                d_a = p.add("S", lambda e: e.activation(out=s1[:, :], in_=vt[:, :], func=AF.Abs), deps=[d_v])
                d_e = p.add("S", lambda e: e.activation(out=s2[:, :], in_=s1[:, :], func=AF.Exp, scale=csts[-1.0]), deps=[d_a])
                d_l = p.add("S", lambda e: e.activation(out=s1[:, :], in_=s2[:, :], func=AF.Ln, bias=csts[1.0]), deps=[d_e])
                d_r = p.add("V", lambda e: e.tensor_scalar(out=s2[:, :], in0=vt[:, :], scalar1=0.0, scalar2=None, op0=ALU.max), deps=[d_l, d_v])
                d_d = p.add("V", lambda e: e.tensor_tensor(out=cacc[:, :], in0=s2[:, :], in1=s1[:, :], op=ALU.add), deps=[d_r, d_l])
                d_prev_out = p.add("Q", (lambda ii: lambda e: e.dma_start(
                    out=delta_out[ii * 128:(ii + 1) * 128, :], in_=cacc[:, :]))(i), deps=[d_d], dma=True)

            p.add("G", lambda e: e.engine_nop(), deps=[d_prev_out, d_uo, d_bco])
            p.build(nc, block)
    return nc


def build_L2():
    nc = bass.Bass("TRN2", target_bir_lowering=False)
    dfm = nc.dram_tensor("dfm", [DSL, L], F32, kind="ExternalInput").ap()
    ufm = nc.dram_tensor("ufm", [DSL, L], F32, kind="ExternalInput").ap()
    zfm = nc.dram_tensor("zfm", [DSL, L], F32, kind="ExternalInput").ap()
    bct = nc.dram_tensor("bct", [32, L], F32, kind="ExternalInput").ap()
    alog = nc.dram_tensor("alog", [DSL, NS], F32, kind="ExternalInput").ap()
    dsl = nc.dram_tensor("dsl", [DSL, 1], F32, kind="ExternalInput").ap()
    woutT = nc.dram_tensor("woutT", [DSL, DM], F32, kind="ExternalInput").ap()

    attn_part = nc.dram_tensor("attn_part", [L, DM], F32, kind="ExternalOutput").ap()

    Q = 1024
    NQ = L // Q

    p = Prog()
    with ExitStack() as stk:
        def sb(name, shape, dt):
            return stk.enter_context(nc.sbuf_tensor(name, shape, dt))

        def ps(name, shape, dt):
            return stk.enter_context(nc.psum_tensor(name, shape, dt))

        ones1 = sb("ones1", [1, 128], F32)
        rows = [sb(f"rows{i}", [1, 2048], F32) for i in range(2)]
        A_sb = sb("A_sb", [128, 4 * NS], F32)
        D_sb = sb("D_sb", [128, 4], F32)
        wo_sb = sb("wo_sb", [128, 4 * DM], BF16)
        carry = sb("carry", [128, 4 * NS], F32)
        dq = sb("dq", [128, Q], F32)
        uq = sb("uq", [128, Q], F32)
        zq = sb("zq", [128, Q], F32)
        wq = sb("wq", [128, Q], F32)
        yq = sb("yq", [128, Q], F32)
        dAq = sb("dAq", [128, Q], F32)
        dbu = sb("dbu", [128, Q], F32)
        stq = sb("stq", [128, Q], F32)
        tmpq = sb("tmpq", [128, Q], F32)
        sgq = sb("sgq", [128, Q], F32)
        ybf = sb("ybf", [128, 4 * L], BF16)
        oev = sb("oev", [128, DM], F32)

        pB = ps("pB", [128, Q], F32)
        pC = ps("pC", [128, Q], F32)
        pO = ps("pO", [128, DM], F32)

        with nc.Block() as block:
            d_ones = p.add("V", lambda e: e.memset(ones1[:], 1.0))
            d_wo = p.add("G", lambda e: e.dma_start(out=wo_sb[:, :].rearrange("p (g m) -> p g m", g=4),
                                                    in_=woutT.rearrange("(g p) m -> p g m", p=128)), dma=True)
            d_al = p.add("Q", lambda e: e.dma_start(out=A_sb[:, :].rearrange("p (g n) -> p g n", g=4),
                                                    in_=alog.rearrange("(g p) n -> p g n", p=128)), dma=True)
            d_dl = p.add("Q", lambda e: e.dma_start(out=D_sb[:, :].rearrange("p (g o) -> p g o", o=1),
                                                    in_=dsl.rearrange("(g p) o -> p g o", p=128),
                                                    allow_slow_non_contiguous=True), dma=True)
            d_A = p.add("S", lambda e: e.activation(out=A_sb[:, :], in_=A_sb[:, :], func=AF.Exp), deps=[d_al])
            d_A = p.add("V", lambda e: e.tensor_scalar(out=A_sb[:, :], in0=A_sb[:, :], scalar1=-1.0, scalar2=None, op0=ALU.mult), deps=[d_A])
            d_cz = p.add("V", lambda e: e.memset(carry[:], 0.0))

            d_prev = {}
            d_rows = None
            d_db_prev = None
            d_tm_prev = None
            for g in range(4):
                for q in range(NQ):
                    deps0 = [d_A, d_cz, d_prev.get((g, q - 1)), d_prev.get((g - 1, NQ - 1))]
                    d_ld = p.add("Q", (lambda gg, qq: lambda e: e.dma_start(
                        out=dq[:, :], in_=dfm[gg * 128:(gg + 1) * 128, qq * Q:(qq + 1) * Q]))(g, q), deps=deps0, dma=True)
                    d_lu = p.add("Q", (lambda gg, qq: lambda e: e.dma_start(
                        out=uq[:, :], in_=ufm[gg * 128:(gg + 1) * 128, qq * Q:(qq + 1) * Q]))(g, q), deps=deps0, dma=True)
                    d_lz = p.add("G", (lambda gg, qq: lambda e: e.dma_start(
                        out=zq[:, :], in_=zfm[gg * 128:(gg + 1) * 128, qq * Q:(qq + 1) * Q]))(g, q), deps=deps0, dma=True)
                    d_w = p.add("V", lambda e: e.tensor_tensor(out=wq[:, :], in0=dq[:, :], in1=uq[:, :], op=ALU.mult), deps=[d_ld, d_lu])
                    d_y = p.add("V", (lambda gg: lambda e: e.tensor_scalar(
                        out=yq[:, :], in0=uq[:, :], scalar1=D_sb[:, gg:gg + 1], scalar2=None, op0=ALU.mult))(g), deps=[d_lu, d_dl])
                    for n in range(NS):
                        pp = n % 2
                        rb = rows[pp]
                        d_rows = p.add("G", (lambda nn, qq, b2: lambda e: e.dma_start(
                            out=b2[:, :].rearrange("p (o l) -> p o l", o=2),
                            in_=bct.rearrange("(h n) l -> n h l", n=16)[nn:nn + 1, :, qq * Q:(qq + 1) * Q]))(n, q, rb),
                            deps=[d_prev.get(("mC", pp))], dma=True)
                        d_dA = p.add("S", (lambda gg, nn: lambda e: e.activation(
                            out=dAq[:, :], in_=dq[:, :], func=AF.Exp,
                            scale=A_sb[:, gg * NS + nn:gg * NS + nn + 1]))(g, n),
                            deps=[d_ld, d_A, d_prev.get(("sc", g, q, n - 1))])
                        d_mB = p.add("T", (lambda b2: lambda e: e.matmul(
                            pB[:, 0:512], ones1[:, :], b2[0:1, 0:512],
                            start=True, stop=True))(rb), deps=[d_rows, d_ones, d_db_prev])
                        d_mB = p.add("T", (lambda b2: lambda e: e.matmul(
                            pB[:, 512:1024], ones1[:, :], b2[0:1, 512:1024],
                            start=True, stop=True))(rb), deps=[d_mB])
                        d_db = p.add("V", lambda e: e.tensor_tensor(out=dbu[:, :], in0=wq[:, :], in1=pB[:, :], op=ALU.mult), deps=[d_w, d_mB])
                        d_db_prev = d_db
                        d_sc = p.add("V", (lambda gg, nn: lambda e: e.tensor_tensor_scan(
                            out=stq[:, :], data0=dAq[:, :], data1=dbu[:, :],
                            initial=carry[:, gg * NS + nn:gg * NS + nn + 1],
                            op0=ALU.mult, op1=ALU.add))(g, n), deps=[d_dA, d_db])
                        d_prev[("sc", g, q, n)] = d_sc
                        d_cu = p.add("V", (lambda gg, nn: lambda e: e.tensor_copy(
                            carry[:, gg * NS + nn:gg * NS + nn + 1], stq[:, Q - 1:Q]))(g, n), deps=[d_sc])
                        d_mC = p.add("T", (lambda b2: lambda e: e.matmul(
                            pC[:, 0:512], ones1[:, :], b2[0:1, 1024:1536],
                            start=True, stop=True))(rb), deps=[d_rows, d_ones, d_tm_prev])
                        d_mC = p.add("T", (lambda b2: lambda e: e.matmul(
                            pC[:, 512:1024], ones1[:, :], b2[0:1, 1536:2048],
                            start=True, stop=True))(rb), deps=[d_mC])
                        d_prev[("mC", pp)] = d_mC
                        d_tm = p.add("V", lambda e: e.tensor_tensor(out=tmpq[:, :], in0=stq[:, :], in1=pC[:, :], op=ALU.mult), deps=[d_sc, d_mC])
                        d_tm_prev = d_tm
                        d_y = p.add("V", lambda e: e.tensor_tensor(out=yq[:, :], in0=yq[:, :], in1=tmpq[:, :], op=ALU.add), deps=[d_tm, d_y])
                    d_sg = p.add("S", lambda e: e.activation(out=sgq[:, :], in_=zq[:, :], func=AF.Sigmoid), deps=[d_lz])
                    d_sz = p.add("V", lambda e: e.tensor_tensor(out=sgq[:, :], in0=zq[:, :], in1=sgq[:, :], op=ALU.mult), deps=[d_sg])
                    d_yg = p.add("V", lambda e: e.tensor_tensor(out=tmpq[:, :], in0=yq[:, :], in1=sgq[:, :], op=ALU.mult), deps=[d_y, d_sz])
                    d_prev[(g, q)] = p.add("V", (lambda gg, qq: lambda e: e.tensor_copy(
                        ybf[:, gg * L + qq * Q: gg * L + (qq + 1) * Q], tmpq[:, :]))(g, q), deps=[d_yg])

            d_done = [d_prev[(g, NQ - 1)] for g in range(4)]
            d_oprev = None
            for tt in range(64):
                d_mm = None
                for g in range(4):
                    for h in range(2):
                        d_mm = p.add("T", (lambda gg, hh, t2: lambda e: e.matmul(
                            pO[:, hh * 512:(hh + 1) * 512],
                            ybf[:, gg * L + t2 * 128: gg * L + t2 * 128 + 128],
                            wo_sb[:, gg * DM + hh * 512: gg * DM + (hh + 1) * 512],
                            start=(gg == 0), stop=(gg == 3)))(g, h, tt),
                            deps=d_done + [d_wo, d_oprev])
                d_ev = p.add("S", lambda e: e.copy(out=oev[:, :], in_=pO[:, :]), deps=[d_mm])
                d_oprev = p.add("Q", (lambda t2: lambda e: e.dma_start(
                    out=attn_part[t2 * 128:(t2 + 1) * 128, :], in_=oev[:, :]))(tt), deps=[d_ev], dma=True)

            p.add("G", lambda e: e.engine_nop(), deps=[d_oprev])
            p.build(nc, block)
    return nc


def build_L3():
    nc = bass.Bass("TRN2", target_bir_lowering=False)
    attn4 = nc.dram_tensor("attn4", [4, TL, DM], F32, kind="ExternalInput").ap()
    x_loc = nc.dram_tensor("x_loc", [TL, DM], F32, kind="ExternalInput").ap()
    rwT = nc.dram_tensor("rwT", [DM, 8], F32, kind="ExternalInput").ap()

    xmid_out = nc.dram_tensor("xmid_out", [TL, DM], F32, kind="ExternalOutput").ap()
    xf_out = nc.dram_tensor("xf_out", [TL, DM], BF16, kind="ExternalOutput").ap()
    logitsT_out = nc.dram_tensor("logitsT_out", [8, TL], F32, kind="ExternalOutput").ap()

    p = Prog()
    with ExitStack() as stk:
        def sb(name, shape, dt):
            return stk.enter_context(nc.sbuf_tensor(name, shape, dt))

        def ps(name, shape, dt):
            return stk.enter_context(nc.psum_tensor(name, shape, dt))

        rw_sb = sb("rw_sb", [128, 8 * 8], BF16)
        a0 = sb("a0", [128, DM], F32)
        a1 = sb("a1", [128, DM], F32)
        xt = sb("xt", [128, DM], F32)
        lnt = sb("lnt", [128, DM], F32)
        xft = sb("xft", [128, DM], F32)
        xfb = sb("xfb", [128, DM], BF16)
        xnb = sb("xnb", [128, DM], BF16)
        xnT = sb("xnT", [128, DM], BF16)
        sumt = sb("sumt", [128, 1], F32)
        sqt = sb("sqt", [128, DM], F32)
        ssqt = sb("ssqt", [128, 1], F32)
        meant = sb("meant", [128, 1], F32)
        vart = sb("vart", [128, 1], F32)
        rstdt = sb("rstdt", [128, 1], F32)
        st_ = sb("st_", [128, 1], F32)
        lgev = sb("lgev", [8, 128], F32)

        ptr = ps("ptr", [128, 128], BF16)
        plg = ps("plg", [8, 128], F32)

        with nc.Block() as block:
            ident, identf, d_id, d_idf = ident_setup(p, nc, sb)
            csts, d_cst = const_cols(p, sb, [LNEPS])
            d_rw = p.add("G", lambda e: e.dma_start(out=rw_sb[:, :].rearrange("p (c o) -> p c o", c=8),
                                                    in_=rwT.rearrange("(c p) o -> p c o", p=128)), dma=True)

            lnscr = (sumt, sqt, ssqt, vart, meant, rstdt)
            d_prev = None
            for i in range(16):
                deps0 = [d_prev]
                d_l0 = p.add("Q", (lambda ii: lambda e: e.dma_start(out=a0[:, :], in_=attn4[0, ii * 128:(ii + 1) * 128, :]))(i), deps=deps0, dma=True)
                d_l1 = p.add("G", (lambda ii: lambda e: e.dma_start(out=a1[:, :], in_=attn4[1, ii * 128:(ii + 1) * 128, :]))(i), deps=deps0, dma=True)
                d_s = p.add("V", lambda e: e.tensor_tensor(out=a0[:, :], in0=a0[:, :], in1=a1[:, :], op=ALU.add), deps=[d_l0, d_l1])
                d_l2 = p.add("G", (lambda ii: lambda e: e.dma_start(out=a1[:, :], in_=attn4[2, ii * 128:(ii + 1) * 128, :]))(i), deps=[d_s], dma=True)
                d_s = p.add("V", lambda e: e.tensor_tensor(out=a0[:, :], in0=a0[:, :], in1=a1[:, :], op=ALU.add), deps=[d_l2])
                d_l3 = p.add("G", (lambda ii: lambda e: e.dma_start(out=a1[:, :], in_=attn4[3, ii * 128:(ii + 1) * 128, :]))(i), deps=[d_s], dma=True)
                d_s = p.add("V", lambda e: e.tensor_tensor(out=a0[:, :], in0=a0[:, :], in1=a1[:, :], op=ALU.add), deps=[d_l3])
                d_lx = p.add("Q", (lambda ii: lambda e: e.dma_start(out=xt[:, :], in_=x_loc[ii * 128:(ii + 1) * 128, :]))(i), deps=deps0, dma=True)
                d_ln = ln_tile(p, a0[:, :], lnscr, lnt[:, :], [d_s, d_cst], csts[LNEPS])
                d_xm = p.add("V", lambda e: e.tensor_tensor(out=xt[:, :], in0=xt[:, :], in1=lnt[:, :], op=ALU.add), deps=[d_ln, d_lx])
                d_xo = p.add("Q", (lambda ii: lambda e: e.dma_start(out=xmid_out[ii * 128:(ii + 1) * 128, :], in_=xt[:, :]))(i), deps=[d_xm], dma=True)
                d_l3n = ln_tile(p, xt[:, :], lnscr, xft[:, :], [d_xm], csts[LNEPS])
                d_fb = p.add("V", lambda e: e.tensor_copy(xfb[:, :], xft[:, :]), deps=[d_l3n])
                d_fo = p.add("Q", (lambda ii: lambda e: e.dma_start(out=xf_out[ii * 128:(ii + 1) * 128, :], in_=xfb[:, :]))(i), deps=[d_fb], dma=True)
                d_q = p.add("S", lambda e: e.activation(out=sqt[:, :], in_=xft[:, :], func=AF.Square), deps=[d_l3n])
                d_qs = p.add("V", lambda e: e.tensor_reduce(out=ssqt[:, :], in_=sqt[:, :], axis=AX, op=ALU.add), deps=[d_q])
                d_m = p.add("V", lambda e: e.tensor_scalar(out=st_[:, :], in0=ssqt[:, :], scalar1=1.0 / DM, scalar2=RMSEPS, op0=ALU.mult, op1=ALU.add), deps=[d_qs])
                d_sq = p.add("S", lambda e: e.activation(out=st_[:, :], in_=st_[:, :], func=AF.Sqrt), deps=[d_m])
                d_rc = p.add("V", lambda e: e.reciprocal(out=st_[:, :], in_=st_[:, :]), deps=[d_sq])
                d_xn = p.add("V", lambda e: e.tensor_scalar(out=xnb[:, :], in0=xft[:, :], scalar1=st_[:, :], scalar2=None, op0=ALU.mult), deps=[d_rc, d_l3n])
                lastev = None
                d_trs = []
                for c in range(8):
                    d_t = p.add("T", (lambda cc: lambda e: e.transpose(
                        ptr[:, :], xnb[:, cc * 128:(cc + 1) * 128], ident[:, :]))(c), deps=[d_xn, d_id, lastev])
                    lastev = p.add("S", (lambda cc: lambda e: e.copy(
                        out=xnT[:, cc * 128:(cc + 1) * 128], in_=ptr[:, 0:128]))(c), deps=[d_t])
                    d_trs.append(lastev)
                d_mm = None
                for c in range(8):
                    d_mm = p.add("T", (lambda cc: lambda e: e.matmul(
                        plg[:, :], rw_sb[:, cc * 8:(cc + 1) * 8], xnT[:, cc * 128:(cc + 1) * 128],
                        start=(cc == 0), stop=(cc == 7)))(c), deps=[d_trs[c], d_rw])
                d_le = p.add("S", lambda e: e.copy(out=lgev[:, :], in_=plg[:, :]), deps=[d_mm])
                d_prev = p.add("Q", (lambda ii: lambda e: e.dma_start(
                    out=logitsT_out[:, ii * 128:(ii + 1) * 128], in_=lgev[:, :]))(i), deps=[d_le, d_xo, d_fo], dma=True)

            p.add("G", lambda e: e.engine_nop(), deps=[d_prev])
            p.build(nc, block)
    return nc


def build_L4(debug=False):
    nc = bass.Bass("TRN2", target_bir_lowering=False)
    xf_full = nc.dram_tensor("xf_full", [T, DM], BF16, kind="ExternalInput").ap()
    logits_full = nc.dram_tensor("logits_full", [T, 8], F32, kind="ExternalInput").ap()
    e_col = nc.dram_tensor("e_col", [128, 1], F32, kind="ExternalInput").ap()
    rg_e = nc.dram_tensor("rg_e", [DM, 4096], F32, kind="ExternalInput").ap()
    ru_e = nc.dram_tensor("ru_e", [DM, 4096], F32, kind="ExternalInput").ap()
    rd_e = nc.dram_tensor("rd_e", [4096, DM], F32, kind="ExternalInput").ap()

    routed = nc.dram_tensor("routed", [T + 128, DM], F32, kind="ExternalOutput").ap()
    if debug:
        dbg_rank = nc.dram_tensor("dbg_rank", [128, 128], F32, kind="ExternalOutput").ap()
        dbg_we = nc.dram_tensor("dbg_we", [128, 128], F32, kind="ExternalOutput").ap()
        dbg_srk = nc.dram_tensor("dbg_srk", [128, 128], F32, kind="ExternalOutput").ap()
        dbg_tok = nc.dram_tensor("dbg_tok", [128, NGRP], F32, kind="ExternalOutput").ap()
        dbg_wsl = nc.dram_tensor("dbg_wsl", [128, NGRP], F32, kind="ExternalOutput").ap()
        dbg_am1 = nc.dram_tensor("dbg_am1", [128, 128], F32, kind="ExternalOutput").ap()
        dbg_am2 = nc.dram_tensor("dbg_am2", [128, 128], F32, kind="ExternalOutput").ap()
        dbg_xgT = nc.dram_tensor("dbg_xgT", [128, 8 * CAP], BF16, kind="ExternalOutput").ap()

    rg_bf = nc.dram_tensor("rg_bf", [DM, 4096], BF16, kind="Internal").ap()
    ru_bf = nc.dram_tensor("ru_bf", [DM, 4096], BF16, kind="Internal").ap()
    rd_bf = nc.dram_tensor("rd_bf", [4096, DM], BF16, kind="Internal").ap()
    slotmap = nc.dram_tensor("slotmap", [CAP, 16], F32, kind="Internal").ap()

    p = Prog()
    with ExitStack() as stk:
        def sb(name, shape, dt):
            return stk.enter_context(nc.sbuf_tensor(name, shape, dt))

        def ps(name, shape, dt):
            return stk.enter_context(nc.psum_tensor(name, shape, dt))

        ec = sb("ec", [128, 1], F32)
        lg = sb("lg", [128, 1024], F32)
        m1 = sb("m1", [128, 128], F32)
        m2 = sb("m2", [128, 128], F32)
        am1 = sb("am1", [128, 128], F32)
        am2 = sb("am2", [128, 128], F32)
        eq = sb("eq", [128, 1024], F32)
        sel9 = sb("sel9", [128, 1024], F32)
        lgm = sb("lgm", [128, 1024], F32)
        e2 = sb("e2", [128, 128], F32)
        rw1 = sb("rw1", [128, 128], F32)
        rw2 = sb("rw2", [128, 128], F32)
        we = sb("we", [128, 128], F32)
        sele = sb("sele", [128, 128], F32)
        incl = sb("incl", [128, 128], F32)
        rank = sb("rank", [128, 128], F32)
        valid = sb("valid", [128, 128], F32)
        srk = sb("srk", [128, 128], F32)
        srk32 = sb("srk32", [128, 128], U32)
        tokid = sb("tokid", [128, 128], F32)
        io8 = sb("io8", [128, 1024], F32)
        io8m = sb("io8m", [128, 1024], F32)
        ones128 = sb("ones128", [128, 128], F32)
        ones_r = sb("ones_r", [1, 128], F32)
        rsrow = sb("rsrow", [1, 128], F32)
        rsincl = sb("rsincl", [1, 128], F32)
        offc = sb("offc", [128, 1], F32)
        scat = sb("scat", [128, 128 * 16], F32)
        slotsb = sb("slotsb", [128, NGRP * 16], F32)
        tok_sl = sb("tok_sl", [128, NGRP], F32)
        w_sl = sb("w_sl", [128, NGRP], F32)
        tok32 = sb("tok32", [128, NGRP], U32)
        sidxf = sb("sidxf", [128, NGRP], F32)
        sidx32 = sb("sidx32", [128, NGRP], U32)
        trcol = sb("trcol", [128, 1], F32)
        cn99 = sb("cn99", [128, 1], F32)
        cn3e9 = sb("cn3e9", [128, 1], F32)
        sltmp = sb("sltmp", [128, NGRP], F32)
        xg = sb("xg", [128, DM], BF16)
        xgT = sb("xgT", [128, 8 * CAP], BF16)
        zero_sb = sb("zero_sb", [128, DM], F32)
        rgt = [sb(f"rgt{i}", [128, 8 * 128], BF16) for i in range(2)]
        rut = [sb(f"rut{i}", [128, 8 * 128], BF16) for i in range(2)]
        rdt = [sb(f"rdt{i}", [128, DM], BF16) for i in range(2)]
        gu_tb = sb("gu_tb", [128, 32 * 512], BF16)
        gtmp = sb("gtmp", [128, 512], F32)
        contrib = sb("contrib", [128, DM], F32)

        ptr = ps("ptr", [128, 128], BF16)
        ptc = ps("ptc", [128, 128], F32)
        pg = ps("pg", [128, 512], F32)
        pu = ps("pu", [128, 512], F32)
        pd = [ps(f"pd{i}", [128, DM], F32) for i in range(2)]

        regs = {}

        def init_regs(e):
            regs["cap"] = e.alloc_register(name="rcap")
            regs["tok"] = e.alloc_register(name="rtok")
            e.reg_mov(regs["cap"], CAP - 1)
            return e.reg_mov(regs["tok"], T - 1)

        with nc.Block() as block:
            d_regs = p.add("G", init_regs)
            ident, identf, d_id, d_idf = ident_setup(p, nc, sb)
            d_cg = p.add("G", lambda e: e.dma_start(out=rg_bf, in_=rg_e), dma=True)
            d_cu = p.add("G", lambda e: e.dma_start(out=ru_bf, in_=ru_e), dma=True)
            d_cd = p.add("G", lambda e: e.dma_start(out=rd_bf, in_=rd_e), dma=True)
            d_zs = p.add("V", lambda e: e.memset(zero_sb[:], 0.0))
            d_zero = []
            for i in range(129):
                d_zero.append(p.add("Q", (lambda ii: lambda e: e.dma_start(
                    out=routed[ii * 128:(ii + 1) * 128, :], in_=zero_sb[:, :]))(i), deps=[d_zs], dma=True))
            d_zslot = p.add("Q", lambda e: e.dma_start(
                out=slotmap.rearrange("(p r) c -> p (r c)", p=128), in_=zero_sb[:, 0:CAP * 16 // 128]), deps=[d_zs], dma=True)

            d_tk = p.add("G", lambda e: e.iota(tokid[:, :], pattern=[[1, 128]], base=0, channel_multiplier=128, allow_small_or_imprecise_dtypes=True))
            d_tr = p.add("G", lambda e: e.iota(trcol[:, :], pattern=[[1, 1]], base=TRASH, channel_multiplier=1, allow_small_or_imprecise_dtypes=True))
            d_io8 = p.add("G", lambda e: e.iota(io8[:, :], pattern=[[0, 128], [1, 8]], base=0, channel_multiplier=0, allow_small_or_imprecise_dtypes=True))
            d_io8m = p.add("V", lambda e: e.tensor_scalar(out=io8m[:, :], in0=io8[:, :], scalar1=-99.0, scalar2=None, op0=ALU.add), deps=[d_io8])
            d_or = p.add("V", lambda e: e.memset(ones_r[:], 1.0))
            d_o128 = p.add("V", lambda e: e.memset(ones128[:], 1.0))
            d_c99 = p.add("V", lambda e: e.memset(cn99[:], 99.0))
            d_c3e9 = p.add("V", lambda e: e.memset(cn3e9[:], 3.0e9))
            d_ec = p.add("Q", lambda e: e.dma_start(out=ec[:], in_=e_col), dma=True)

            d_lg = p.add("Q", lambda e: e.dma_start(out=lg[:, :], in_=logits_full.rearrange("(p c) o -> p (c o)", p=128)), dma=True)
            lg3 = lg[:, :].rearrange("p (c o) -> p c o", o=8)
            eq3 = eq[:, :].rearrange("p (c o) -> p c o", o=8)
            sel93 = sel9[:, :].rearrange("p (c o) -> p c o", o=8)
            lgm3 = lgm[:, :].rearrange("p (c o) -> p c o", o=8)
            io83 = io8[:, :].rearrange("p (c o) -> p c o", o=8)

            def b3(t):
                return t[:, :].rearrange("p (c o) -> p c o", o=1).broadcast_to((128, 128, 8))

            d_m1 = p.add("V", lambda e: e.tensor_reduce(out=m1[:, :], in_=lg3, axis=AX, op=ALU.max), deps=[d_lg])
            d_eq = p.add("V", lambda e: e.tensor_tensor(out=eq3, in0=lg3, in1=b3(m1), op=ALU.is_equal), deps=[d_m1])
            d_s9 = p.add("V", lambda e: e.tensor_tensor(out=sel93, in0=eq3, in1=io8m[:, :].rearrange("p (c o) -> p c o", o=8), op=ALU.mult), deps=[d_eq, d_io8m])
            d_s9 = p.add("V", lambda e: e.tensor_scalar(out=sel9[:, :], in0=sel9[:, :], scalar1=99.0, scalar2=None, op0=ALU.add), deps=[d_s9])
            d_a1 = p.add("V", lambda e: e.tensor_reduce(out=am1[:, :], in_=sel93, axis=AX, op=ALU.min), deps=[d_s9])
            d_q1 = p.add("V", lambda e: e.tensor_tensor(out=eq3, in0=io83, in1=b3(am1), op=ALU.is_equal), deps=[d_a1, d_io8])
            d_lm = p.add("V", lambda e: e.scalar_tensor_tensor(out=lgm3, in0=eq3, scalar=-1e30, in1=lg3, op0=ALU.mult, op1=ALU.add), deps=[d_q1, d_lg])
            d_m2 = p.add("V", lambda e: e.tensor_reduce(out=m2[:, :], in_=lgm3, axis=AX, op=ALU.max), deps=[d_lm])
            d_eq2 = p.add("V", lambda e: e.tensor_tensor(out=eq3, in0=lgm3, in1=b3(m2), op=ALU.is_equal), deps=[d_m2, d_lm])
            d_s92 = p.add("V", lambda e: e.tensor_tensor(out=sel93, in0=eq3, in1=io8m[:, :].rearrange("p (c o) -> p c o", o=8), op=ALU.mult), deps=[d_eq2, d_io8m])
            d_s92 = p.add("V", lambda e: e.tensor_scalar(out=sel9[:, :], in0=sel9[:, :], scalar1=99.0, scalar2=None, op0=ALU.add), deps=[d_s92])
            d_a2 = p.add("V", lambda e: e.tensor_reduce(out=am2[:, :], in_=sel93, axis=AX, op=ALU.min), deps=[d_s92])
            d_d21 = p.add("V", lambda e: e.tensor_tensor(out=e2[:, :], in0=m2[:, :], in1=m1[:, :], op=ALU.subtract), deps=[d_m2, d_m1])
            d_ex = p.add("S", lambda e: e.activation(out=e2[:, :], in_=e2[:, :], func=AF.Exp), deps=[d_d21])
            d_dn = p.add("V", lambda e: e.tensor_scalar(out=rw1[:, :], in0=e2[:, :], scalar1=1.0, scalar2=None, op0=ALU.add), deps=[d_ex])
            d_r1 = p.add("V", lambda e: e.reciprocal(out=rw1[:, :], in_=rw1[:, :]), deps=[d_dn])
            d_r2 = p.add("V", lambda e: e.tensor_tensor(out=rw2[:, :], in0=e2[:, :], in1=rw1[:, :], op=ALU.mult), deps=[d_r1, d_ex])
            d_e1 = p.add("V", lambda e: e.tensor_scalar(out=we[:, :], in0=am1[:, :], scalar1=ec[:, :], scalar2=None, op0=ALU.is_equal), deps=[d_a1, d_ec])
            d_w1 = p.add("V", lambda e: e.tensor_tensor(out=we[:, :], in0=we[:, :], in1=rw1[:, :], op=ALU.mult), deps=[d_e1, d_r1])
            d_e2m = p.add("V", lambda e: e.tensor_scalar(out=sele[:, :], in0=am2[:, :], scalar1=ec[:, :], scalar2=None, op0=ALU.is_equal), deps=[d_a2, d_ec])
            d_w2 = p.add("V", lambda e: e.tensor_tensor(out=sele[:, :], in0=sele[:, :], in1=rw2[:, :], op=ALU.mult), deps=[d_e2m, d_r2])
            d_we = p.add("V", lambda e: e.tensor_tensor(out=we[:, :], in0=we[:, :], in1=sele[:, :], op=ALU.add), deps=[d_w1, d_w2])
            d_se = p.add("V", lambda e: e.tensor_scalar(out=sele[:, :], in0=we[:, :], scalar1=0.0, scalar2=None, op0=ALU.is_gt), deps=[d_we])
            d_ic = p.add("V", lambda e: e.tensor_tensor_scan(out=incl[:, :], data0=ones128[:, :], data1=sele[:, :], initial=0.0, op0=ALU.mult, op1=ALU.add), deps=[d_se, d_o128])
            d_rx = p.add("V", lambda e: e.tensor_tensor(out=rank[:, :], in0=incl[:, :], in1=sele[:, :], op=ALU.subtract), deps=[d_ic])
            d_tp = p.add("T", lambda e: e.transpose(ptc[0:1, 0:128], incl[:, 127:128], identf[:, :]), deps=[d_ic, d_idf])
            d_rs = p.add("S", lambda e: e.copy(out=rsrow[:, :], in_=ptc[0:1, 0:128]), deps=[d_tp])
            d_ip = p.add("V", lambda e: e.tensor_tensor_scan(out=rsincl[:, :], data0=ones_r[:, :], data1=rsrow[:, :], initial=0.0, op0=ALU.mult, op1=ALU.add), deps=[d_rs, d_or])
            d_ep = p.add("V", lambda e: e.tensor_tensor(out=rsincl[:, :], in0=rsincl[:, :], in1=rsrow[:, :], op=ALU.subtract), deps=[d_ip])
            d_tb = p.add("T", lambda e: e.transpose(ptc[:, 0:1], rsincl[0:1, :], identf[0:1, 0:1]), deps=[d_ep, d_rs])
            d_of = p.add("S", lambda e: e.copy(out=offc[:, :], in_=ptc[:, 0:1]), deps=[d_tb])
            d_rk = p.add("V", lambda e: e.tensor_scalar(out=rank[:, :], in0=rank[:, :], scalar1=offc[:, :], scalar2=None, op0=ALU.add), deps=[d_rx, d_of])
            d_lt = p.add("V", lambda e: e.tensor_scalar(out=valid[:, :], in0=rank[:, :], scalar1=float(CAP), scalar2=None, op0=ALU.is_lt), deps=[d_rk])
            d_vd = p.add("V", lambda e: e.tensor_tensor(out=valid[:, :], in0=valid[:, :], in1=sele[:, :], op=ALU.mult), deps=[d_lt, d_se])
            d_sk = p.add("V", lambda e: e.tensor_scalar(out=srk[:, :], in0=rank[:, :], scalar1=-1.0e6, scalar2=None, op0=ALU.add), deps=[d_rk])
            d_sk = p.add("V", lambda e: e.tensor_tensor(out=srk[:, :], in0=srk[:, :], in1=valid[:, :], op=ALU.mult), deps=[d_sk, d_vd])
            d_sk = p.add("V", lambda e: e.tensor_scalar(out=srk[:, :], in0=srk[:, :], scalar1=1.0e6, scalar2=None, op0=ALU.add), deps=[d_sk])
            d_sk32 = p.add("V", lambda e: e.tensor_copy(srk32[:, :], srk[:, :]), deps=[d_sk])
            scat3 = scat[:, :].rearrange("p (c o) -> p c o", o=16)
            d_sc0 = p.add("V", lambda e: e.tensor_copy(scat3[:, :, 0:1], tokid[:, :].rearrange("p (c o) -> p c o", o=1)), deps=[d_tk])
            d_sc1 = p.add("V", lambda e: e.tensor_copy(scat3[:, :, 1:2], we[:, :].rearrange("p (c o) -> p c o", o=1)), deps=[d_we, d_sc0])
            d_scats = []
            for c in range(128):
                d_scats.append(p.add("G", (lambda cc: lambda e: e.indirect_dma_start(
                    out=slotmap, out_offset=bass.IndirectOffsetOnAxis(ap=srk32[:, cc:cc + 1], axis=0),
                    in_=scat[:, cc * 16:(cc + 1) * 16],
                    in_offset=None, bounds_check=regs["cap"], oob_is_err=False))(c),
                    deps=[d_sk32, d_sc1, d_zslot], dma=True))

            d_sl = p.add("Q", lambda e: e.dma_start(
                out=slotsb[:, :].rearrange("p (g o) -> p g o", o=16),
                in_=slotmap.rearrange("(g p) o -> p g o", p=128)), deps=d_scats, dma=True)
            slot3 = slotsb[:, :].rearrange("p (g o) -> p g o", o=16)
            d_ts = p.add("V", lambda e: e.tensor_copy(tok_sl[:, :].rearrange("p (g o) -> p g o", o=1), slot3[:, :, 0:1]), deps=[d_sl])
            d_ws = p.add("V", lambda e: e.tensor_copy(w_sl[:, :].rearrange("p (g o) -> p g o", o=1), slot3[:, :, 1:2]), deps=[d_sl])
            d_t32 = p.add("V", lambda e: e.tensor_copy(tok32[:, :], tok_sl[:, :]), deps=[d_ts])
            d_vg = p.add("V", lambda e: e.tensor_scalar(out=sltmp[:, :], in0=w_sl[:, :], scalar1=0.0, scalar2=None, op0=ALU.is_gt), deps=[d_ws])
            d_si = p.add("V", lambda e: e.tensor_tensor(out=sidxf[:, :], in0=tok_sl[:, :], in1=trcol[:, :].broadcast_to((128, NGRP)), op=ALU.subtract), deps=[d_ts, d_tr])
            d_si = p.add("V", lambda e: e.tensor_tensor(out=sidxf[:, :], in0=sidxf[:, :], in1=sltmp[:, :], op=ALU.mult), deps=[d_si, d_vg])
            d_si = p.add("V", lambda e: e.tensor_tensor(out=sidxf[:, :], in0=sidxf[:, :], in1=trcol[:, :].broadcast_to((128, NGRP)), op=ALU.add), deps=[d_si])
            d_s32 = p.add("V", lambda e: e.tensor_copy(sidx32[:, :], sidxf[:, :]), deps=[d_si])

            if debug:
                p.add("Q", lambda e: e.dma_start(out=dbg_rank, in_=rank[:, :]), deps=[d_rk], dma=True)
                p.add("Q", lambda e: e.dma_start(out=dbg_we, in_=we[:, :]), deps=[d_we], dma=True)
                p.add("Q", lambda e: e.dma_start(out=dbg_srk, in_=srk[:, :]), deps=[d_sk], dma=True)
                p.add("Q", lambda e: e.dma_start(out=dbg_tok, in_=tok_sl[:, :]), deps=[d_ts], dma=True)
                p.add("Q", lambda e: e.dma_start(out=dbg_wsl, in_=w_sl[:, :]), deps=[d_ws], dma=True)
                p.add("Q", lambda e: e.dma_start(out=dbg_am1, in_=am1[:, :]), deps=[d_a1], dma=True)
                p.add("Q", lambda e: e.dma_start(out=dbg_am2, in_=am2[:, :]), deps=[d_a2], dma=True)
            lastev = None
            for c in range(NGRP):
                d_g = p.add("G", (lambda cc: lambda e: e.indirect_dma_start(
                    out=xg[:, :], out_offset=None, in_=xf_full,
                    in_offset=bass.IndirectOffsetOnAxis(ap=tok32[:, cc:cc + 1], axis=0),
                    bounds_check=regs["tok"], oob_is_err=False))(c), deps=[d_t32, lastev], dma=True)
                for dmc in range(8):
                    d_t = p.add("T", (lambda dd: lambda e: e.transpose(
                        ptr[:, :], xg[:, dd * 128:(dd + 1) * 128], ident[:, :]))(dmc), deps=[d_g, d_id, lastev])
                    lastev = p.add("S", (lambda dd, cc: lambda e: e.copy(
                        out=xgT[:, dd * CAP + cc * 128: dd * CAP + (cc + 1) * 128], in_=ptr[:, :]))(dmc, c), deps=[d_t])
            d_xgT = lastev
            if debug:
                p.add("Q", lambda e: e.dma_start(out=dbg_xgT, in_=xgT[:, :]), deps=[d_xgT], dma=True)

            d_last_sc = None
            for tb in range(5):
                d_gu = {}
                for irc in range(32):
                    pp = irc % 2
                    war = [d_gu.get(irc - 2)]
                    d_lrg = p.add("Q", (lambda ii, b2: lambda e: e.dma_start(
                        out=rgt[b2][:, :].rearrange("p (c o) -> p c o", c=8),
                        in_=rg_bf[:, ii * 128:(ii + 1) * 128].rearrange("(c p) o -> p c o", p=128)))(irc, pp),
                        deps=[d_cg] + war, dma=True)
                    d_lru = p.add("G", (lambda ii, b2: lambda e: e.dma_start(
                        out=rut[b2][:, :].rearrange("p (c o) -> p c o", c=8),
                        in_=ru_bf[:, ii * 128:(ii + 1) * 128].rearrange("(c p) o -> p c o", p=128)))(irc, pp),
                        deps=[d_cu] + war, dma=True)
                    d_mg = d_mu = None
                    warp = [d_gu.get(irc - 1)]
                    for dmc in range(8):
                        d_mg = p.add("T", (lambda dd, bb, b2: lambda e: e.matmul(
                            pg[:, :], rgt[b2][:, dd * 128:(dd + 1) * 128],
                            xgT[:, dd * CAP + bb * 512:dd * CAP + (bb + 1) * 512],
                            start=(dd == 0), stop=(dd == 7)))(dmc, tb, pp), deps=[d_lrg, d_xgT] + (warp if dmc == 0 else []))
                        d_mu = p.add("T", (lambda dd, bb, b2: lambda e: e.matmul(
                            pu[:, :], rut[b2][:, dd * 128:(dd + 1) * 128],
                            xgT[:, dd * CAP + bb * 512:dd * CAP + (bb + 1) * 512],
                            start=(dd == 0), stop=(dd == 7)))(dmc, tb, pp), deps=[d_lru] + (warp if dmc == 0 else []))
                    d_sg = p.add("S", lambda e: e.activation(out=gtmp[:, :], in_=pg[:, :], func=AF.Silu), deps=[d_mg] + warp)
                    d_gu[irc] = p.add("V", (lambda ii: lambda e: e.tensor_tensor(
                        out=gu_tb[:, ii * 512:(ii + 1) * 512], in0=gtmp[:, :], in1=pu[:, :], op=ALU.mult))(irc), deps=[d_sg, d_mu])
                for ts2 in range(4):
                    cslot = tb * 4 + ts2
                    pdx = pd[ts2 % 2]
                    d_md = None
                    d_md_list = []
                    for irc in range(32):
                        pp = irc % 2
                        d_lrd = p.add("Q" if irc % 2 == 0 else "G", (lambda ii, b2: lambda e: e.dma_start(
                            out=rdt[b2][:, :], in_=rd_bf[ii * 128:(ii + 1) * 128, :]))(irc, pp),
                            deps=[d_cd, d_gu[irc], d_md_list[-2] if len(d_md_list) >= 2 else None, d_last_sc], dma=True)
                        for h in range(2):
                            d_md = p.add("T", (lambda ii, hh, t2, b2: lambda e: e.matmul(
                                pdx[:, hh * 512:(hh + 1) * 512],
                                gu_tb[:, ii * 512 + t2 * 128: ii * 512 + (t2 + 1) * 128],
                                rdt[b2][:, hh * 512:(hh + 1) * 512],
                                start=(ii == 0), stop=(ii == 31)))(irc, h, ts2, pp), deps=[d_lrd])
                        d_md_list.append(d_md)
                    d_ct = p.add("V", (lambda cc: lambda e: e.tensor_scalar(
                        out=contrib[:, :], in0=pdx[:, :], scalar1=w_sl[:, cc:cc + 1], scalar2=None, op0=ALU.mult))(cslot),
                        deps=[d_md, d_ws, d_last_sc])
                    d_last_sc = p.add("G", (lambda cc: lambda e: e.indirect_dma_start(
                        out=routed, out_offset=bass.IndirectOffsetOnAxis(ap=sidx32[:, cc:cc + 1], axis=0),
                        in_=contrib[:, :], in_offset=None, oob_is_err=True, compute_op=ALU.add))(cslot),
                        deps=[d_ct, d_s32] + d_zero, dma=True)

            p.add("V", lambda e: e.engine_nop(), deps=[d_last_sc])
            p.build(nc, block)
    return nc


def build_L5():
    nc = bass.Bass("TRN2", target_bir_lowering=False)
    xmid = nc.dram_tensor("xmid", [TL, DM], F32, kind="ExternalInput").ap()
    xf_bf = nc.dram_tensor("xf_bf", [TL, DM], BF16, kind="ExternalInput").ap()
    routed8 = nc.dram_tensor("routed8", [8, TL, DM], F32, kind="ExternalInput").ap()
    sg_w = nc.dram_tensor("sg_w", [DM, L], F32, kind="ExternalInput").ap()
    su_w = nc.dram_tensor("su_w", [DM, L], F32, kind="ExternalInput").ap()
    sd_w = nc.dram_tensor("sd_w", [L, DM], F32, kind="ExternalInput").ap()

    out = nc.dram_tensor("out", [TL, DM], F32, kind="ExternalOutput").ap()

    sg_bf = nc.dram_tensor("sg_bf", [DM, L], BF16, kind="Internal").ap()
    su_bf = nc.dram_tensor("su_bf", [DM, L], BF16, kind="Internal").ap()
    sd_bf = nc.dram_tensor("sd_bf", [L, DM], BF16, kind="Internal").ap()

    p = Prog()
    with ExitStack() as stk:
        def sb(name, shape, dt):
            return stk.enter_context(nc.sbuf_tensor(name, shape, dt))

        def ps(name, shape, dt):
            return stk.enter_context(nc.psum_tensor(name, shape, dt))

        xfb = sb("xfb", [128, DM], BF16)
        xfT = sb("xfT", [128, 8 * TL], BF16)
        sgt = [sb(f"sgt{i}", [128, 8 * 128], BF16) for i in range(2)]
        sut = [sb(f"sut{i}", [128, 8 * 128], BF16) for i in range(2)]
        sdt = [sb(f"sdt{i}", [128, DM], BF16) for i in range(2)]
        gu_tb = sb("gu_tb", [128, 64 * 512], BF16)
        gtmp = sb("gtmp", [128, 512], F32)
        acc = sb("acc", [128, DM], F32)
        r1t = sb("r1t", [128, DM], F32)
        xmt = sb("xmt", [128, DM], F32)
        lnt = sb("lnt", [128, DM], F32)
        sumt = sb("sumt", [128, 1], F32)
        sqt = sb("sqt", [128, DM], F32)
        ssqt = sb("ssqt", [128, 1], F32)
        meant = sb("meant", [128, 1], F32)
        vart = sb("vart", [128, 1], F32)
        rstdt = sb("rstdt", [128, 1], F32)

        ptr = ps("ptr", [128, 128], BF16)
        pg = ps("pg", [128, 512], F32)
        pu = ps("pu", [128, 512], F32)
        pd = [ps(f"pd{i}", [128, DM], F32) for i in range(2)]

        with nc.Block() as block:
            ident, identf, d_id, d_idf = ident_setup(p, nc, sb)
            csts, d_cst = const_cols(p, sb, [LNEPS])
            d_csg = p.add("G", lambda e: e.dma_start(out=sg_bf, in_=sg_w), dma=True)
            d_csu = p.add("G", lambda e: e.dma_start(out=su_bf, in_=su_w), dma=True)
            d_csd = p.add("G", lambda e: e.dma_start(out=sd_bf, in_=sd_w), dma=True)

            lastev = None
            for i in range(16):
                d_lx = p.add("Q", (lambda ii: lambda e: e.dma_start(out=xfb[:, :], in_=xf_bf[ii * 128:(ii + 1) * 128, :]))(i),
                             deps=[lastev], dma=True)
                for c in range(8):
                    d_t = p.add("T", (lambda cc: lambda e: e.transpose(
                        ptr[:, :], xfb[:, cc * 128:(cc + 1) * 128], ident[:, :]))(c), deps=[d_lx, d_id, lastev])
                    lastev = p.add("S", (lambda cc, ii: lambda e: e.copy(
                        out=xfT[:, cc * TL + ii * 128: cc * TL + (ii + 1) * 128], in_=ptr[:, :]))(c, i), deps=[d_t])
            d_xT = lastev

            lnscr = (sumt, sqt, ssqt, vart, meant, rstdt)
            d_prev_out = None
            for tb in range(4):
                d_gu = {}
                for ihc in range(64):
                    pp = ihc % 2
                    war = [d_gu.get(ihc - 2)]
                    d_lg = p.add("Q", (lambda ii, b2: lambda e: e.dma_start(
                        out=sgt[b2][:, :].rearrange("p (c o) -> p c o", c=8),
                        in_=sg_bf[:, ii * 128:(ii + 1) * 128].rearrange("(c p) o -> p c o", p=128)))(ihc, pp),
                        deps=[d_csg] + war, dma=True)
                    d_lu = p.add("G", (lambda ii, b2: lambda e: e.dma_start(
                        out=sut[b2][:, :].rearrange("p (c o) -> p c o", c=8),
                        in_=su_bf[:, ii * 128:(ii + 1) * 128].rearrange("(c p) o -> p c o", p=128)))(ihc, pp),
                        deps=[d_csu] + war, dma=True)
                    d_mg = d_mu = None
                    warp = [d_gu.get(ihc - 1)]
                    for dmc in range(8):
                        d_mg = p.add("T", (lambda dd, bb, b2: lambda e: e.matmul(
                            pg[:, :], sgt[b2][:, dd * 128:(dd + 1) * 128],
                            xfT[:, dd * TL + bb * 512: dd * TL + (bb + 1) * 512],
                            start=(dd == 0), stop=(dd == 7)))(dmc, tb, pp), deps=[d_lg, d_xT] + (warp if dmc == 0 else []))
                        d_mu = p.add("T", (lambda dd, bb, b2: lambda e: e.matmul(
                            pu[:, :], sut[b2][:, dd * 128:(dd + 1) * 128],
                            xfT[:, dd * TL + bb * 512: dd * TL + (bb + 1) * 512],
                            start=(dd == 0), stop=(dd == 7)))(dmc, tb, pp), deps=[d_lu] + (warp if dmc == 0 else []))
                    d_sg2 = p.add("S", lambda e: e.activation(out=gtmp[:, :], in_=pg[:, :], func=AF.Silu), deps=[d_mg] + warp)
                    d_gu[ihc] = p.add("V", (lambda ii: lambda e: e.tensor_tensor(
                        out=gu_tb[:, ii * 512:(ii + 1) * 512], in0=gtmp[:, :], in1=pu[:, :], op=ALU.mult))(ihc), deps=[d_sg2, d_mu])
                for ts2 in range(4):
                    ti = tb * 4 + ts2
                    pdx = pd[ts2 % 2]
                    d_md = None
                    d_md_list = []
                    for ihc in range(64):
                        pp = ihc % 2
                        d_lrd = p.add("Q" if ihc % 2 == 0 else "G", (lambda ii, b2: lambda e: e.dma_start(
                            out=sdt[b2][:, :], in_=sd_bf[ii * 128:(ii + 1) * 128, :]))(ihc, pp),
                            deps=[d_csd, d_gu[ihc], d_md_list[-2] if len(d_md_list) >= 2 else None, d_prev_out], dma=True)
                        for h in range(2):
                            d_md = p.add("T", (lambda ii, hh, t2, b2: lambda e: e.matmul(
                                pdx[:, hh * 512:(hh + 1) * 512],
                                gu_tb[:, ii * 512 + t2 * 128: ii * 512 + (t2 + 1) * 128],
                                sdt[b2][:, hh * 512:(hh + 1) * 512],
                                start=(ii == 0), stop=(ii == 63)))(ihc, h, ts2, pp), deps=[d_lrd])
                        d_md_list.append(d_md)
                    d_ac = p.add("V", lambda e: e.tensor_copy(acc[:, :], pdx[:, :]), deps=[d_md])
                    for ee in range(8):
                        d_lr = p.add("Q" if ee % 2 == 0 else "G", (lambda eee, t2: lambda e: e.dma_start(
                            out=r1t[:, :], in_=routed8[eee, t2 * 128:(t2 + 1) * 128, :]))(ee, ti),
                            deps=[d_ac], dma=True)
                        d_ac = p.add("V", lambda e: e.tensor_tensor(out=acc[:, :], in0=acc[:, :], in1=r1t[:, :], op=ALU.add), deps=[d_lr])
                    d_lm = p.add("Q", (lambda t2: lambda e: e.dma_start(out=xmt[:, :], in_=xmid[t2 * 128:(t2 + 1) * 128, :]))(ti),
                                 deps=[d_prev_out], dma=True)
                    d_ln = ln_tile(p, acc[:, :], lnscr, lnt[:, :], [d_ac, d_cst], csts[LNEPS])
                    d_fo = p.add("V", lambda e: e.tensor_tensor(out=xmt[:, :], in0=xmt[:, :], in1=lnt[:, :], op=ALU.add), deps=[d_ln, d_lm])
                    d_prev_out = p.add("Q", (lambda t2: lambda e: e.dma_start(
                        out=out[t2 * 128:(t2 + 1) * 128, :], in_=xmt[:, :]))(ti), deps=[d_fo], dma=True)

            p.add("V", lambda e: e.engine_nop(), deps=[d_prev_out])
            p.build(nc, block)
    return nc


_NC_CACHE = {}


def _get(name, builder):
    if name not in _NC_CACHE:
        _NC_CACHE[name] = builder()
    return _NC_CACHE[name]


def kernel(**inp):
    x = np.asarray(inp["L0_attn_hidden_states"])
    xf_flat = np.ascontiguousarray(x.reshape(T, DM))
    w_inT = np.ascontiguousarray(np.asarray(inp["L0_attn_in_proj_w"]).T)
    convwT = np.ascontiguousarray(np.asarray(inp["L0_attn_conv_w"])[:, 0, :].T)
    conv_b = np.ascontiguousarray(np.asarray(inp["L0_attn_conv_b"]).reshape(1, DI))
    xprojT = np.ascontiguousarray(np.asarray(inp["L0_attn_x_proj_w"]).T)
    dtwT = np.ascontiguousarray(np.asarray(inp["L0_attn_dt_proj_w"]).T)
    dtb = np.ascontiguousarray(np.asarray(inp["L0_attn_dt_proj_b"]).reshape(1, DI))
    alog = np.ascontiguousarray(np.asarray(inp["L0_attn_A_log"]))
    Dv = np.ascontiguousarray(np.asarray(inp["L0_attn_D"]).reshape(DI, 1))
    woutT = np.ascontiguousarray(np.asarray(inp["L0_attn_out_proj_w"]).T)
    rwT = np.ascontiguousarray(np.asarray(inp["L0_ffn_router_w"]).T)
    rg = np.asarray(inp["L0_ffn_rg"])
    ru = np.asarray(inp["L0_ffn_ru"])
    rd = np.asarray(inp["L0_ffn_rd"])
    sg = np.ascontiguousarray(np.asarray(inp["L0_ffn_sg"]))
    su = np.ascontiguousarray(np.asarray(inp["L0_ffn_su"]))
    sd = np.ascontiguousarray(np.asarray(inp["L0_ffn_sd"]))

    cores = list(range(NCORES))

    in1 = []
    for c in cores:
        lo = c * TL
        halo = np.zeros((3, DM), np.float32) if c % 4 == 0 else xf_flat[lo - 3:lo]
        xh = np.ascontiguousarray(np.concatenate(
            [halo, xf_flat[lo:lo + TL], np.zeros((125, DM), np.float32)], 0))
        in1.append(dict(x_halo=xh, w_inT=w_inT, convwT=convwT, conv_b=conv_b,
                        xprojT=xprojT, dtwT=dtwT, dtb=dtb))
    r1 = run_bass_kernel_spmd(_get("L1", build_L1), in1, core_ids=cores).results

    in2 = []
    for c2 in cores:
        b = c2 // 4
        dq = c2 % 4
        cs = [4 * b + j for j in range(4)]
        dfm = np.ascontiguousarray(
            np.concatenate([r1[j]["delta_out"] for j in cs], 0).T[dq * DSL:(dq + 1) * DSL])
        ufm = np.ascontiguousarray(
            np.concatenate([r1[j]["u_out"] for j in cs], 0).T[dq * DSL:(dq + 1) * DSL])
        zfm = np.ascontiguousarray(
            np.concatenate([r1[j]["z_buf"][3:3 + TL] for j in cs], 0).T[dq * DSL:(dq + 1) * DSL])
        bct = np.ascontiguousarray(np.concatenate([r1[j]["bc_out"] for j in cs], 1))
        in2.append(dict(dfm=dfm, ufm=ufm, zfm=zfm, bct=bct,
                        alog=np.ascontiguousarray(alog[dq * DSL:(dq + 1) * DSL]),
                        dsl=np.ascontiguousarray(Dv[dq * DSL:(dq + 1) * DSL]),
                        woutT=np.ascontiguousarray(woutT[dq * DSL:(dq + 1) * DSL])))
    r2 = run_bass_kernel_spmd(_get("L2", build_L2), in2, core_ids=cores).results

    in3 = []
    for c in cores:
        b = c // 4
        lq = c % 4
        attn4 = np.ascontiguousarray(np.stack(
            [r2[4 * b + dq]["attn_part"][lq * TL:(lq + 1) * TL] for dq in range(4)], 0))
        in3.append(dict(attn4=attn4, x_loc=np.ascontiguousarray(xf_flat[c * TL:(c + 1) * TL]), rwT=rwT))
    r3 = run_bass_kernel_spmd(_get("L3", build_L3), in3, core_ids=cores).results

    xf_full = np.ascontiguousarray(np.concatenate([r3[c]["xf_out"] for c in cores], 0))
    logits_full = np.ascontiguousarray(
        np.concatenate([r3[c]["logitsT_out"] for c in cores], 1).T.astype(np.float32))
    in4 = []
    for c in cores:
        in4.append(dict(xf_full=xf_full, logits_full=logits_full,
                        e_col=np.full((128, 1), float(c), np.float32),
                        rg_e=np.ascontiguousarray(rg[c]),
                        ru_e=np.ascontiguousarray(ru[c]),
                        rd_e=np.ascontiguousarray(rd[c])))
    r4 = run_bass_kernel_spmd(_get("L4", build_L4), in4, core_ids=cores).results

    in5 = []
    for c in cores:
        routed8 = np.ascontiguousarray(np.stack(
            [r4[e]["routed"][c * TL:(c + 1) * TL] for e in cores], 0))
        in5.append(dict(xmid=np.ascontiguousarray(r3[c]["xmid_out"]),
                        xf_bf=np.ascontiguousarray(r3[c]["xf_out"]),
                        routed8=routed8, sg_w=sg, su_w=su, sd_w=sd))
    r5 = run_bass_kernel_spmd(_get("L5", build_L5), in5, core_ids=cores).results

    out = np.concatenate([r5[c]["out"] for c in cores], 0)
    return np.ascontiguousarray(out.reshape(2, L, DM).astype(np.float32))
